# version 05
# baseline (speedup 1.0000x reference)
import numpy as np
import ml_dtypes
from contextlib import ExitStack

import concourse.bass as bass
import concourse.mybir as mybir
from concourse.bass_utils import run_bass_kernel_spmd

F32 = mybir.dt.float32
BF16 = mybir.dt.bfloat16
AF = mybir.ActivationFunctionType
ALU = mybir.AluOpType
AXX = mybir.AxisListType.X

C = 2048
TK = 2048
KC = 16
HC = 64
AR = 1024
EPS = 1e-5

DEBUG = False
DEBUG2 = False

ENG_SEM = {"tensor": "pe", "vector": "ve", "scalar": "se", "gpsimd": "ge"}
FIFO_ENGINES = ("tensor", "vector", "scalar")


class Res:
    __slots__ = ("w", "r")

    def __init__(self, preload=()):
        self.w = None
        self.r = list(preload)


class Prog:
    def __init__(self, nc, sems):
        self.nc = nc
        self.sems = sems
        self.tot = {s: 0 for s in sems}
        self.stream = {e: [] for e in
                       ["sync", "tensor", "vector", "scalar", "gpsimd"]}
        self.hw = {e: {} for e in self.stream}

    def _waits(self, eng, reads, writes):
        need = {}
        for r in reads:
            if r.w:
                need.setdefault(r.w[0], []).append(r.w)
        for w in writes:
            for ev in ([w.w] if w.w else []) + w.r:
                need.setdefault(ev[0], []).append(ev)
        out = []
        for s, evs in need.items():
            if (eng in FIFO_ENGINES and all(ev[2] == eng for ev in evs)):
                continue
            v = max(ev[1] for ev in evs)
            if self.hw[eng].get(s, 0) >= v:
                continue
            out.append((s, v))
        return out

    def op(self, eng, fns, reads=(), writes=(), sem=None, by=1, fifo=True):
        if callable(fns):
            fns = [fns]
        reads = list(reads)
        writes = list(writes)
        sem = sem or ENG_SEM[eng]
        waits = self._waits(eng, reads, writes)
        for s, v in waits:
            assert v <= self.tot[s], f"forward wait {s}:{v}>{self.tot[s]}"
            self.hw[eng][s] = max(self.hw[eng].get(s, 0), v)
        self.tot[sem] += by
        ev = (sem, self.tot[sem], eng if fifo else "*")
        self.stream[eng].append((fns, waits, sem, by))
        for r in reads:
            r.r.append(ev)
        for w in writes:
            w.w = ev
            w.r = []
        return ev

    def dma(self, q, out, in_, reads=(), writes=(), slow=False):
        sem = {"sync": "wq", "gpsimd": "gq", "scalar": "sq"}[q]
        if slow:
            fn = lambda E, o=out, i=in_: E.dma_start(
                out=o, in_=i, allow_slow_non_contiguous=True)
        else:
            fn = lambda E, o=out, i=in_: E.dma_start(out=o, in_=i)
        return self.op(q, fn, reads=reads, writes=writes, sem=sem, by=16)

    def barrier_res(self, *groups):
        evs = []
        for g in groups:
            for r in (g if isinstance(g, (list, tuple)) else [g]):
                if r.w:
                    evs.append(r.w)
                evs.extend(r.r)
        return Res(preload=evs)


def build_program(debug=False, debug2=False):
    nc = bass.Bass("TRN2", target_bir_lowering=False, debug=False)

    din = {}
    for nm, shp in [
        ("x0", [KC, 128, TK]),
        ("wk", [KC, 128, KC, 128]), ("wv", [KC, 128, KC, 128]),
        ("wr", [KC, 128, KC, 128]), ("wo", [KC, 128, KC, 128]),
        ("wfk", [HC, 128, KC, 128]), ("wfv", [KC, 128, HC, 128]),
        ("wg", [HC, 128, KC, 128]), ("wu", [HC, 128, KC, 128]),
        ("wd", [KC, 128, HC, 128]),
        ("td", [128, KC]), ("tf", [128, KC]),
        ("ln1w", [128, KC]), ("ln1b", [128, KC]),
        ("ln2w", [128, KC]), ("ln2b", [128, KC]),
        ("zd", [128, TK]),
        ("epow", [128, 4]), ("cm512", [128, 512]), ("bmask", [128, 2]),
        ("ones_bf", [128, 1]), ("ones_f", [33, 128]),
    ]:
        dt = BF16 if nm in ("ones_bf", "zd", "cm512") else F32
        din[nm] = nc.dram_tensor(nm, shp, dt, kind="ExternalInput").ap()

    o_x = nc.dram_tensor("o_x", [KC, 128, TK], F32, kind="ExternalOutput").ap()
    dbg = {}
    if debug2:
        for nm, shp, dt in [
            ("d2_ccin", [KC, 128], BF16),
            ("d2_ccout", [4, KC, 128], BF16),
            ("d2_C16", [128, KC], F32),
            ("d2_Ff", [128, 64], BF16),
            ("d2_wA", [128, 64], BF16),
            ("d2_wC", [128, 64], BF16),
            ("d2_netd", [128, KC], F32),
        ]:
            dbg[nm] = nc.dram_tensor(nm, shp, dt, kind="ExternalOutput").ap()
    if debug:
        for nm, shp, dt in [
            ("dbg_states", [128, KC * TK], BF16),
            ("dbg_rwkv", [128, KC * TK], BF16),
            ("dbg_xl2", [128, KC * TK], BF16),
            ("dbg_f", [128, KC], F32),
            ("dbg_C", [128, KC], F32),
            ("dbg_xn", [KC * 128 * TK], BF16),
            ("dbg_Ff", [128, 64], BF16),
            ("dbg_xl", [128, KC * TK], BF16),
        ]:
            dbg[nm] = nc.dram_tensor(nm, shp, dt, kind="ExternalOutput").ap()

    d_xn = nc.dram_tensor("d_xn", [KC, 128, TK], BF16).ap()
    d_a1 = nc.dram_tensor("d_a1", [KC, 128, TK], F32).ap()
    d_gated = nc.dram_tensor("d_gated", [HC, 128, TK], BF16).ap()
    d_h2 = nc.dram_tensor("d_h2", [HC, 128, TK], BF16).ap()
    d_g = nc.dram_tensor("d_g", [KC, 128, TK], F32).ap()
    cc_in = nc.dram_tensor("cc_in", [KC, 128], BF16)
    cc_out = nc.dram_tensor("cc_out", [4, KC, 128], BF16)

    stack = ExitStack()
    e = stack.enter_context

    sb_st = e(nc.sbuf_tensor("sb_st", [128, KC * TK], BF16))
    sb_ac = e(nc.sbuf_tensor("sb_ac", [128, KC * TK], BF16))
    st_v = sb_st.ap().rearrange("p (m t) -> p m t", m=KC)
    ac_v = sb_ac.ap().rearrange("p (m t) -> p m t", m=KC)
    st_h = sb_st.ap().rearrange("p (h t) -> p h t", h=32)
    ac_h = sb_ac.ap().rearrange("p (h t) -> p h t", h=32)

    sb_wst = e(nc.sbuf_tensor("sb_wst", [128, 4096], F32))
    sb_wbf = e(nc.sbuf_tensor("sb_wbf", [128, 8192], BF16))
    wst_v = sb_wst.ap().rearrange("p (b k) -> p b k", b=2)
    wbf_v = sb_wbf.ap().rearrange("p (b k) -> p b k", b=4)

    sb_tmp = e(nc.sbuf_tensor("sb_tmp", [128, TK], F32))
    sb_tmp2 = e(nc.sbuf_tensor("sb_tmp2", [128, TK], F32))
    sb_bft = e(nc.sbuf_tensor("sb_bft", [128, TK], BF16))
    sb_bft2 = e(nc.sbuf_tensor("sb_bft2", [128, TK], BF16))
    sb_lam = e(nc.sbuf_tensor("sb_lam", [128, TK], F32))
    u_ap = [sb_lam.ap()[:, 0:AR], sb_lam.ap()[:, AR:2 * AR]]
    x0b_ap = [sb_tmp.ap()[:, AR:2 * AR], sb_tmp2.ap()[:, AR:2 * AR]]

    sb_sm = e(nc.sbuf_tensor("sb_sm", [128, 80], F32))
    sm = sb_sm.ap()
    S_TD, S_TF, S_ETD, S_NETD, S_LAM, S_ETF, S_V0, S_CAR = (
        0, 16, 32, 48, 64, 80, 96, 112)
    S_C, S_T1 = 0, 16
    sb_F = e(nc.sbuf_tensor("sb_F", [128, 512], BF16))
    sb_sml = e(nc.sbuf_tensor("sb_sml", [128, 16], F32))
    sb_ep = sb_sml.ap()[:, 0:4]
    sb_cm = sb_sml.ap()[:, 4:8]
    sb_w4h = sb_sml.ap()[:, 8:12]
    sb_bm = sb_sml.ap()[:, 12:13]
    sb_zero = sb_sml.ap()[:, 13:14]
    sb_ln = e(nc.sbuf_tensor("sb_ln", [128, 4 * KC], F32))
    ln_v = sb_ln.ap().rearrange("p (a k) -> p a k", a=4)
    sb_o1 = e(nc.sbuf_tensor("sb_o1", [128, 1], BF16))
    sb_wA = e(nc.sbuf_tensor("sb_wA", [128, 512], BF16))
    sb_wC = e(nc.sbuf_tensor("sb_wC", [128, 512], BF16))
    sb_cm64 = e(nc.sbuf_tensor("sb_cm64", [128, 512], BF16))
    sb_zd = e(nc.sbuf_tensor("sb_zd", [128, TK], BF16))
    sb_C16 = e(nc.sbuf_tensor("sb_C16", [128, KC], F32))
    sb_etf16 = e(nc.sbuf_tensor("sb_etf16", [128, KC], F32))
    sb_v016 = e(nc.sbuf_tensor("sb_v016", [128, KC], F32))
    sb_boost = e(nc.sbuf_tensor("sb_boost", [128, KC], F32))
    sb_o1f = e(nc.sbuf_tensor("sb_o1f", [33, 128], F32))

    pst = [e(nc.psum_tensor(f"ps{i}", [128, 1024], F32)) for i in range(4)]
    POOLS = [p.ap() for p in pst]
    mm0, mm1, ax0, ax1 = POOLS

    sems = {s: e(nc.semaphore(s))
            for s in ["wq", "gq", "sq", "pe", "ve", "se", "ge", "cc"]}
    P = Prog(nc, sems)

    PS = [Res() for _ in range(4)]
    WST = [Res(), Res()]
    WSL = [Res() for _ in range(4)]
    ST = [Res() for _ in range(KC)]
    AXN = [Res() for _ in range(KC)]
    AXL = [Res() for _ in range(KC)]
    TMP, TMP2, BFT, BFT2, LAM, U, U2 = (Res() for _ in range(7))
    IOTA, ROWS, SMALL, W4, FRES = (Res() for _ in range(5))
    DXN, DXL, DA1, DGATED, DH2, DG, CCIO = (Res() for _ in range(7))

    cast_alt = [0]

    def load_slab(wdram, mc, kc0, nkc, stage, slot, cast_eng=None):
        src = wdram[mc, :, kc0:kc0 + nkc, :]
        dst = wst_v[:, stage, 0:nkc * 128].rearrange("p (k f) -> p k f", k=nkc)
        P.dma("sync", dst, src, writes=[WST[stage]])
        stg = wst_v[:, stage, 0:nkc * 128]
        bfdst = wbf_v[:, slot, 0:nkc * 128]
        if cast_eng is None:
            cast_eng = "gpsimd" if cast_alt[0] == 0 else "vector"
            cast_alt[0] ^= 1
        P.op(cast_eng, lambda E, o=bfdst, i=stg: E.tensor_copy(o, i),
             reads=[WST[stage]], writes=[WSL[slot]])
        return wbf_v[:, slot, :].rearrange("p (k f) -> p k f", k=KC)

    P.dma("gpsimd", sm[:, S_TD:S_TD + 16], din["td"], writes=[SMALL])
    P.dma("gpsimd", sm[:, S_TF:S_TF + 16], din["tf"], writes=[SMALL])
    P.dma("gpsimd", ln_v[:, 0, :], din["ln1w"], writes=[SMALL])
    P.dma("gpsimd", ln_v[:, 1, :], din["ln1b"], writes=[SMALL])
    P.dma("gpsimd", ln_v[:, 2, :], din["ln2w"], writes=[SMALL])
    P.dma("gpsimd", ln_v[:, 3, :], din["ln2b"], writes=[SMALL])
    P.dma("gpsimd", sb_zd.ap(), din["zd"], writes=[IOTA])
    P.dma("gpsimd", sb_ep, din["epow"], writes=[SMALL])
    P.dma("gpsimd", sb_cm64.ap(), din["cm512"], writes=[SMALL])
    P.dma("gpsimd", sb_sml.ap()[:, 12:14], din["bmask"], writes=[SMALL])
    P.dma("gpsimd", sb_o1.ap(), din["ones_bf"], writes=[SMALL])
    P.dma("gpsimd", sb_o1f.ap(), din["ones_f"], writes=[SMALL])

    P.op("scalar", lambda E: E.activation(
        sm[:, S_ETD:S_ETD + 16], sm[:, S_TD:S_TD + 16], AF.Exp),
        reads=[SMALL], writes=[SMALL])
    P.op("vector", lambda E: E.tensor_scalar_mul(
        sm[:, S_NETD:S_NETD + 16], sm[:, S_ETD:S_ETD + 16], -1.0),
        reads=[SMALL], writes=[SMALL])
    P.op("scalar", lambda E: E.activation(
        sm[:, S_LAM:S_LAM + 16], sm[:, S_NETD:S_NETD + 16], AF.Exp),
        reads=[SMALL], writes=[SMALL])
    P.op("scalar", lambda E: E.activation(
        sb_etf16.ap(), sm[:, S_TF:S_TF + 16], AF.Exp),
        reads=[SMALL], writes=[SMALL])
    P.op("vector", lambda E: E.tensor_scalar_mul(
        sb_etf16.ap(), sb_etf16.ap(), sb_bm),
        reads=[SMALL], writes=[SMALL])

    def rows_math(n_tok, sx_pis, sq_pis):
        mu = sb_tmp.ap()[0:1, 0:n_tok]
        va = sb_tmp2.ap()[0:1, 0:n_tok]
        sc = sb_lam.ap()[0:1, 0:n_tok]
        bi = sb_wst.ap()[0:1, 0:n_tok]
        for i, pi in enumerate(sx_pis):
            P.op("vector", lambda E, i=i, pi=pi, mu=mu: E.tensor_scalar_mul(
                mu[0:1, i * 1024:(i + 1) * 1024], POOLS[pi][0:1, :], 1.0 / C),
                reads=[PS[pi]], writes=[TMP])
        for i, pi in enumerate(sq_pis):
            P.op("vector", lambda E, i=i, pi=pi, va=va: E.tensor_scalar_mul(
                va[0:1, i * 1024:(i + 1) * 1024], POOLS[pi][0:1, :], 1.0 / C),
                reads=[PS[pi]], writes=[TMP2])
        P.op("vector", lambda E, sc=sc, mu=mu: E.tensor_mul(sc, mu, mu),
             reads=[TMP, ROWS], writes=[ROWS])
        P.op("vector", lambda E, va=va, sc=sc: E.tensor_sub(va, va, sc),
             reads=[TMP2, ROWS], writes=[TMP2])
        P.op("vector", lambda E, va=va: E.tensor_scalar_add(va, va, EPS),
             reads=[TMP2], writes=[TMP2])
        P.op("scalar", lambda E, va=va: E.activation(va, va, AF.Sqrt),
             reads=[TMP2], writes=[TMP2])
        P.op("vector", lambda E, sc=sc, va=va: E.reciprocal(sc, va),
             reads=[TMP2, ROWS], writes=[ROWS])
        P.op("vector", lambda E, bi=bi, mu=mu, sc=sc: E.tensor_mul(bi, mu, sc),
             reads=[TMP, ROWS, WST[0]], writes=[WST[0]])
        P.op("vector", lambda E, bi=bi: E.tensor_scalar_mul(bi, bi, -1.0),
             reads=[WST[0]], writes=[WST[0]])

    def bcast_rows(n_tok, sc_pi, bi_pi, off=0):
        sc = sb_lam.ap()[0:1, :]
        bi = sb_wst.ap()[0:1, :]
        for s in range(n_tok // 512):
            sl = slice(s * 512, (s + 1) * 512)
            gsl = slice(off + s * 512, off + (s + 1) * 512)
            P.op("tensor", lambda E, sl=sl, gsl=gsl, pi=sc_pi, sc=sc: E.matmul(
                POOLS[pi][:, sl], sb_o1f.ap()[0:1, :], sc[0:1, gsl],
                start=True, stop=True),
                reads=[ROWS, SMALL], writes=[PS[sc_pi]])
            P.op("tensor", lambda E, sl=sl, gsl=gsl, pi=bi_pi, bi=bi: E.matmul(
                POOLS[pi][:, sl], sb_o1f.ap()[0:1, :], bi[0:1, gsl],
                start=True, stop=True),
                reads=[WST[0], SMALL], writes=[PS[bi_pi]])

    x0v = din["x0"]
    for r in range(2):
        t0 = r * AR
        rsl = slice(t0, t0 + AR)
        ROWS = P.barrier_res([ROWS, U, U2])
        for kc in range(KC):
            P.dma("gpsimd", sb_tmp.ap()[:, 0:AR], x0v[kc, :, rsl],
                  writes=[TMP])
            P.op("vector", lambda E: E.tensor_copy(
                sb_bft.ap()[:, 0:AR], sb_tmp.ap()[:, 0:AR]),
                reads=[TMP], writes=[BFT])
            P.op("scalar", lambda E: E.activation(
                sb_bft2.ap()[:, 0:AR], sb_tmp.ap()[:, 0:AR], AF.Square),
                reads=[TMP], writes=[BFT2])
            for s in range(2):
                sl = slice(s * 512, (s + 1) * 512)
                P.op("tensor", lambda E, sl=sl, kc=kc: E.matmul(
                    mm0[0:1, sl], sb_o1.ap(), sb_bft.ap()[:, sl],
                    start=(kc == 0), stop=(kc == KC - 1)),
                    reads=[BFT, SMALL], writes=[PS[0]])
                P.op("tensor", lambda E, sl=sl, kc=kc: E.matmul(
                    ax0[0:1, sl], sb_o1.ap(), sb_bft2.ap()[:, sl],
                    start=(kc == 0), stop=(kc == KC - 1)),
                    reads=[BFT2, SMALL], writes=[PS[2]])
        rows_math(AR, [0], [2])
        bcast_rows(AR, 1, 3)

        for kc in range(KC):
            P.dma("gpsimd", sb_tmp.ap()[:, 0:AR], x0v[kc, :, rsl],
                  writes=[TMP])
            P.op("vector", lambda E: E.tensor_mul(
                sb_tmp2.ap()[:, 0:AR], sb_tmp.ap()[:, 0:AR], mm1[:, :]),
                reads=[TMP, PS[1]], writes=[TMP2])
            P.op("vector", lambda E, kc=kc, rsl=rsl: E.tensor_add(
                st_v[:, kc, rsl], sb_tmp2.ap()[:, 0:AR], ax1[:, :]),
                reads=[TMP2, PS[3]], writes=[AXN[kc]])
            P.dma("gpsimd", d_xn[kc, :, rsl], st_v[:, kc, rsl],
                  reads=[AXN[kc]], writes=[DXN])

        for kc in range(KC):
            P.op("scalar", lambda E, kc=kc, rsl=rsl: E.activation(
                sb_bft2.ap()[:, 0:AR], st_v[:, kc, rsl], AF.Square),
                reads=[AXN[kc]], writes=[BFT2])
            for s in range(2):
                sl = slice(s * 512, (s + 1) * 512)
                gsl = slice(t0 + s * 512, t0 + (s + 1) * 512)
                P.op("tensor", lambda E, sl=sl, kc=kc, gsl=gsl: E.matmul(
                    mm0[0:1, sl], sb_o1.ap(), st_v[:, kc, gsl],
                    start=(kc == 0), stop=(kc == KC - 1)),
                    reads=[AXN[kc], SMALL], writes=[PS[0]])
                P.op("tensor", lambda E, sl=sl, kc=kc: E.matmul(
                    ax0[0:1, sl], sb_o1.ap(), sb_bft2.ap()[:, sl],
                    start=(kc == 0), stop=(kc == KC - 1)),
                    reads=[BFT2, SMALL], writes=[PS[2]])
        rows_math(AR, [0], [2])
        bcast_rows(AR, 1, 3)

        for kc in range(KC):
            P.op("vector", lambda E, kc=kc, rsl=rsl: E.tensor_mul(
                sb_tmp2.ap()[:, 0:AR], st_v[:, kc, rsl], mm1[:, :]),
                reads=[AXN[kc], PS[1]], writes=[TMP2])
            P.op("vector", lambda E: E.tensor_add(
                sb_tmp2.ap()[:, 0:AR], sb_tmp2.ap()[:, 0:AR], ax1[:, :]),
                reads=[TMP2, PS[3]], writes=[TMP2])
            P.op("vector", lambda E, kc=kc, rsl=rsl: E.tensor_scalar(
                ac_v[:, kc, rsl], sb_tmp2.ap()[:, 0:AR],
                ln_v[:, 0, kc:kc + 1], ln_v[:, 1, kc:kc + 1],
                ALU.mult, ALU.add),
                reads=[TMP2, SMALL], writes=[AXL[kc]])

    U = P.barrier_res([U, U2, ROWS])
    for m in range(KC):
        kslab = load_slab(din["wk"], m, 0, KC, 0, 0)
        vslab = load_slab(din["wv"], m, 0, KC, 1, 1)
        ST[m] = P.barrier_res([ST[m], AXN[m]])
        for half in range(2):
            kpool, vpool = (mm0, mm1) if half == 0 else (ax0, ax1)
            kpi, vpi = (0, 1) if half == 0 else (2, 3)
            fns = []
            for kc in range(KC):
                for s in range(2):
                    gsl = slice(half * 1024 + s * 512,
                                half * 1024 + (s + 1) * 512)
                    sl = slice(s * 512, (s + 1) * 512)
                    fns.append(lambda E, kc=kc, sl=sl, gsl=gsl,
                               kp=kpool, ks=kslab: E.matmul(
                                   kp[:, sl], ks[:, kc, :], ac_v[:, kc, gsl],
                                   start=(kc == 0), stop=(kc == KC - 1)))
                for s in range(2):
                    gsl = slice(half * 1024 + s * 512,
                                half * 1024 + (s + 1) * 512)
                    sl = slice(s * 512, (s + 1) * 512)
                    fns.append(lambda E, kc=kc, sl=sl, gsl=gsl,
                               vp=vpool, vs=vslab: E.matmul(
                                   vp[:, sl], vs[:, kc, :], ac_v[:, kc, gsl],
                                   start=(kc == 0), stop=(kc == KC - 1)))
            P.op("tensor", fns, reads=[WSL[0], WSL[1]] + AXL,
                 writes=[PS[kpi], PS[vpi]])
            ktmp, kt_res = (sb_tmp, TMP) if half == 0 else (sb_tmp2, TMP2)
            P.op("scalar", lambda E, kp=kpool, kt=ktmp: E.activation(
                kt.ap()[:, 0:AR], kp[:, :], AF.Exp),
                reads=[PS[kpi]], writes=[kt_res])
            if half == 0:
                P.op("vector", lambda E, vp=vpool, m=m: E.tensor_copy(
                    sb_v016.ap()[:, m:m + 1], vp[:, 0:1]),
                    reads=[PS[vpi]], writes=[SMALL])
            P.op("vector", lambda E, vp=vpool, kt=ktmp, half=half:
                 E.tensor_mul(sb_lam.ap()[:, half * AR:(half + 1) * AR],
                              kt.ap()[:, 0:AR], vp[:, :]),
                 reads=[kt_res, PS[vpi]], writes=[U])
        P.op("vector", lambda E, m=m: E.tensor_tensor_scan(
            st_v[:, m, :],
            sm[:, S_LAM + m:S_LAM + m + 1].broadcast_to([128, TK]),
            sb_lam.ap()[:, 0:TK], 0.0, ALU.mult, ALU.add),
            reads=[U, SMALL], writes=[ST[m]], fifo=False)

    if debug:
        P.dma("gpsimd", dbg["dbg_states"], sb_st.ap(), reads=ST)

    P.dma("gpsimd", cc_in.ap().rearrange("m p -> p m"),
          st_v[:, :, TK - 1:TK].squeeze(), reads=ST, writes=[CCIO], slow=True)
    P.op("gpsimd", lambda E: E.collective_compute(
        "AllGather", ALU.bypass,
        replica_groups=[[0, 1, 2, 3], [4, 5, 6, 7]],
        ins=[cc_in.ap().opt()], outs=[cc_out.ap().opt()]),
        reads=[CCIO], writes=[CCIO], sem="cc")
    if debug:
        P.dma("gpsimd", dbg["dbg_f"], st_v[:, :, TK - 1:TK].squeeze(),
              reads=ST, slow=True)

    ACD = P.barrier_res(AXL)
    if debug:
        P.dma("gpsimd", dbg["dbg_xl"], sb_ac.ap(), reads=[ACD])

    def emit_F_block():
        Fv = sb_F.ap()[:, 0:64].rearrange("p (m r) -> p m r", r=4)
        for rk in range(4):
            P.dma("gpsimd", Fv[:, :, rk],
                  cc_out.ap()[rk].rearrange("m p -> p m"),
                  reads=[CCIO], writes=[FRES], slow=True)
        def spacer():
            P.op("vector", lambda E: E.tensor_copy(
                sb_tmp2.ap()[:, 0:TK], sb_tmp2.ap()[:, 0:TK]),
                reads=[TMP2], writes=[TMP2])
        if debug:
            P.dma("gpsimd", dbg["dbg_Ff"], sb_F.ap()[:, 0:64], reads=[FRES])
        for m in range(KC):
            P.op("scalar", lambda E, m=m: E.activation(
                sb_wA.ap()[:, 4 * m:4 * m + 4], sb_ep, AF.Exp,
                scale=sm[:, S_NETD + m:S_NETD + m + 1]),
                reads=[SMALL], writes=[W4])
        spacer()
        P.op("vector", lambda E: E.tensor_mul(
            sb_wC.ap(), sb_F.ap(), sb_cm64.ap()),
            reads=[FRES, SMALL], writes=[FRES])
        P.op("vector", lambda E: E.tensor_mul(sb_wC.ap(), sb_wC.ap(), sb_wA.ap()),
             reads=[FRES, W4], writes=[FRES])
        spacer()
        P.op("vector", lambda E: E.reduce_sum(
            sb_C16.ap(), sb_wC.ap()[:, 0:64].rearrange("p (m r) -> p m r", r=4),
            AXX),
            reads=[FRES], writes=[SMALL])
        P.op("vector", lambda E: E.tensor_mul(
            sb_boost.ap(), sb_etf16.ap(), sb_v016.ap()),
            reads=[SMALL], writes=[SMALL])
        spacer()
        spacer()
        if debug:
            P.dma("gpsimd", dbg["dbg_C"], sb_C16.ap(), reads=[SMALL])

    LAMF = P.barrier_res([LAM, U, U2, ROWS])

    def mm_full(slab, rhs_fn, par, rhs_res):
        pools = (mm0, mm1) if par == 0 else (ax0, ax1)
        pis = (0, 1) if par == 0 else (2, 3)
        fns = []
        for kc in range(KC):
            for s in range(4):
                pool = pools[s // 2]
                sl = slice((s % 2) * 512, (s % 2 + 1) * 512)
                fns.append(lambda E, kc=kc, s=s, pool=pool, sl=sl, sb=slab:
                           E.matmul(pool[:, sl], sb[:, kc, :], rhs_fn(kc, s),
                                    start=(kc == 0), stop=(kc == KC - 1)))
        return pools, pis, fns

    pend_corr = []
    for m in range(KC):
        if m == 2:
            emit_F_block()
            for f in pend_corr:
                f()
            pend_corr = []
        par = m % 2
        rslab = load_slab(din["wr"], m, 0, KC, par, par * 2, cast_eng="gpsimd")
        pools, pis, fns = mm_full(
            rslab, lambda kc, s: ac_v[:, kc, s * 512:(s + 1) * 512], par, ACD)
        P.op("tensor", fns, reads=[WSL[par * 2], ACD],
             writes=[PS[pis[0]], PS[pis[1]]])
        sig, sres = (sb_bft, BFT) if par == 0 else (sb_bft2, BFT2)
        P.op("scalar", lambda E, p=pools[0], sg=sig: E.activation(
            sg.ap()[:, 0:1024], p[:, :], AF.Sigmoid),
            reads=[PS[pis[0]]], writes=[sres])
        P.op("scalar", lambda E, p=pools[1], sg=sig: E.activation(
            sg.ap()[:, 1024:2048], p[:, :], AF.Sigmoid),
            reads=[PS[pis[1]]], writes=[sres])
        lbuf, lres = (sb_lam, LAMF) if par == 0 else (sb_tmp, TMP)
        P.op("vector", lambda E, m=m, lb=lbuf: E.tensor_tensor_scan(
            lb.ap(), sm[:, S_LAM + m:S_LAM + m + 1].broadcast_to([128, TK]),
            sb_zero.broadcast_to([128, TK]), 1.0, ALU.mult, ALU.add),
            reads=[SMALL], writes=[lres], fifo=False)

        def corr(m=m, lbuf=lbuf, lres=lres, sig=sig, sres=sres):
            P.op("vector", lambda E, m=m, lb=lbuf: E.scalar_tensor_tensor(
                st_v[:, m, :], lb.ap(), sb_C16.ap()[:, m:m + 1],
                sb_st.ap()[:, m * TK:(m + 1) * TK], ALU.mult, ALU.add),
                reads=[lres, SMALL, ST[m]], writes=[ST[m]])
            P.op("vector", lambda E, m=m: E.scalar_tensor_tensor(
                st_v[:, m, :], sb_zd.ap(), sb_boost.ap()[:, m:m + 1],
                sb_st.ap()[:, m * TK:(m + 1) * TK], ALU.mult, ALU.add),
                reads=[SMALL, ST[m]], writes=[ST[m]])
            P.op("vector", lambda E, m=m, sg=sig: E.tensor_mul(
                st_v[:, m, :], st_v[:, m, :], sg.ap()[:, 0:2048]),
                reads=[sres, ST[m]], writes=[ST[m]])

        if m < 2:
            pend_corr.append(corr)
        else:
            corr()
    if debug:
        P.dma("gpsimd", dbg["dbg_rwkv"], sb_st.ap(), reads=ST)

    for m in range(KC):
        par = m % 2
        oslab = load_slab(din["wo"], m, 0, KC, par, par * 2)
        pools, pis, fns = mm_full(
            oslab, lambda kc, s: st_v[:, kc, s * 512:(s + 1) * 512], par, None)
        P.op("tensor", fns, reads=[WSL[par * 2]] + ST,
             writes=[PS[pis[0]], PS[pis[1]]])
        xnb, xres = (sb_bft, BFT) if par == 0 else (sb_bft2, BFT2)
        P.dma("gpsimd", xnb.ap()[:, 0:2048], d_xn[m, :, :],
              reads=[DXN], writes=[xres])
        a1t, ares = (sb_tmp, TMP) if par == 0 else (sb_tmp2, TMP2)
        P.op("vector", lambda E, p=pools[0], a=a1t, xb=xnb: E.tensor_add(
            a.ap()[:, 0:1024], p[:, :], xb.ap()[:, 0:1024]),
            reads=[PS[pis[0]], xres], writes=[ares])
        P.op("vector", lambda E, p=pools[1], a=a1t, xb=xnb: E.tensor_add(
            a.ap()[:, 1024:2048], p[:, :], xb.ap()[:, 1024:2048]),
            reads=[PS[pis[1]], xres], writes=[ares])
        P.dma("gpsimd", d_a1[m, :, :], a1t.ap()[:, 0:2048],
              reads=[ares], writes=[DA1])

    STF = P.barrier_res(ST)
    for kc in range(KC):
        P.dma("gpsimd", sb_tmp.ap()[:, 0:2048], d_a1[kc, :, :],
              reads=[DA1], writes=[TMP])
        P.op("vector", lambda E, kc=kc: E.tensor_copy(
            st_v[:, kc, :], sb_tmp.ap()[:, 0:2048]),
            reads=[TMP], writes=[STF])
        P.op("scalar", lambda E: E.activation(
            sb_bft2.ap()[:, 0:2048], sb_tmp.ap()[:, 0:2048], AF.Square),
            reads=[TMP], writes=[BFT2])
        for s in range(4):
            pxi = s // 2
            sl = slice((s % 2) * 512, (s % 2 + 1) * 512)
            tsl = slice(s * 512, (s + 1) * 512)
            P.op("tensor", lambda E, pxi=pxi, sl=sl, kc=kc, tsl=tsl: E.matmul(
                POOLS[pxi][0:1, sl], sb_o1.ap(), st_v[:, kc, tsl],
                start=(kc == 0), stop=(kc == KC - 1)),
                reads=[STF, SMALL], writes=[PS[pxi]])
            P.op("tensor", lambda E, pxi=pxi, sl=sl, kc=kc, tsl=tsl: E.matmul(
                POOLS[2 + pxi][0:1, sl], sb_o1.ap(), sb_bft2.ap()[:, tsl],
                start=(kc == 0), stop=(kc == KC - 1)),
                reads=[BFT2, SMALL], writes=[PS[2 + pxi]])
    ROWS = P.barrier_res([ROWS, LAMF])
    rows_math(2048, [0, 1], [2, 3])
    for th in range(2):
        tsl = slice(th * 1024, (th + 1) * 1024)
        bcast_rows(1024, 0, 2, off=th * 1024)
        for kc in range(KC):
            P.op("vector", lambda E, kc=kc, tsl=tsl: E.tensor_mul(
                sb_tmp.ap()[:, 0:1024], st_v[:, kc, tsl], mm0[:, :]),
                reads=[STF, PS[0]], writes=[TMP])
            P.op("vector", lambda E: E.tensor_add(
                sb_tmp.ap()[:, 0:1024], sb_tmp.ap()[:, 0:1024], ax0[:, :]),
                reads=[TMP, PS[2]], writes=[TMP])
            P.op("vector", lambda E, kc=kc, tsl=tsl: E.tensor_scalar(
                st_v[:, kc, tsl], sb_tmp.ap()[:, 0:1024],
                ln_v[:, 2, kc:kc + 1], ln_v[:, 3, kc:kc + 1],
                ALU.mult, ALU.add),
                reads=[TMP, SMALL], writes=[STF])
    if debug:
        P.dma("gpsimd", dbg["dbg_xl2"], sb_st.ap(), reads=[STF])

    for h in range(HC):
        par = h % 2
        fslab = load_slab(din["wfk"], h, 0, KC, par, par * 2)
        pools, pis, fns = mm_full(
            fslab, lambda kc, s: st_v[:, kc, s * 512:(s + 1) * 512], par, None)
        P.op("tensor", fns, reads=[WSL[par * 2], STF],
             writes=[PS[pis[0]], PS[pis[1]]])
        tmp, tres = (sb_tmp, TMP) if par == 0 else (sb_tmp2, TMP2)
        hb, hres = (sb_bft, BFT) if par == 0 else (sb_bft2, BFT2)
        P.op("scalar", lambda E, p=pools[0], t=tmp: E.activation(
            t.ap()[:, 0:1024], p[:, :], AF.Relu),
            reads=[PS[pis[0]]], writes=[tres])
        P.op("scalar", lambda E, p=pools[1], t=tmp: E.activation(
            t.ap()[:, 1024:2048], p[:, :], AF.Relu),
            reads=[PS[pis[1]]], writes=[tres])
        P.op("vector", lambda E, t=tmp, hb=hb: E.tensor_mul(
            hb.ap()[:, 0:2048], t.ap()[:, 0:2048], t.ap()[:, 0:2048]),
            reads=[tres], writes=[hres])
        P.dma("gpsimd", d_h2[h, :, :], hb.ap()[:, 0:2048],
              reads=[hres], writes=[DH2])

    ACF = P.barrier_res([ACD])
    for kc in range(KC):
        P.dma("gpsimd", ac_v[:, kc, :], d_xn[kc, :, :],
              reads=[DXN], writes=[ACF])
    for h in range(HC):
        par = h % 2
        gslab = load_slab(din["wg"], h, 0, KC, par, par * 2 + 0)
        uslab = load_slab(din["wu"], h, 0, KC, par, par * 2 + 1)
        for th in range(2):
            pidx = (h * 2 + th) % 2
            gpool, upool = (mm0, ax0) if pidx == 0 else (mm1, ax1)
            gpi, upi = (0, 2) if pidx == 0 else (1, 3)
            fns = []
            for kc in range(KC):
                for s in range(2):
                    sl = slice(s * 512, (s + 1) * 512)
                    rsl = slice(th * 1024 + s * 512, th * 1024 + (s + 1) * 512)
                    fns.append(lambda E, kc=kc, sl=sl, rsl=rsl, gp=gpool,
                               gs=gslab: E.matmul(
                                   gp[:, sl], gs[:, kc, :], ac_v[:, kc, rsl],
                                   start=(kc == 0), stop=(kc == KC - 1)))
                for s in range(2):
                    sl = slice(s * 512, (s + 1) * 512)
                    rsl = slice(th * 1024 + s * 512, th * 1024 + (s + 1) * 512)
                    fns.append(lambda E, kc=kc, sl=sl, rsl=rsl, up=upool,
                               us=uslab: E.matmul(
                                   up[:, sl], us[:, kc, :], ac_v[:, kc, rsl],
                                   start=(kc == 0), stop=(kc == KC - 1)))
            P.op("tensor", fns,
                 reads=[WSL[par * 2], WSL[par * 2 + 1], ACF],
                 writes=[PS[gpi], PS[upi]])
            tmp, tres = (sb_tmp, TMP) if pidx == 0 else (sb_tmp2, TMP2)
            hb, hres = (sb_bft, BFT) if pidx == 0 else (sb_bft2, BFT2)
            P.op("scalar", lambda E, p=gpool, t=tmp: E.activation(
                t.ap()[:, 0:1024], p[:, :], AF.Silu),
                reads=[PS[gpi]], writes=[tres])
            P.op("vector", lambda E, p=upool, t=tmp, hb=hb: E.tensor_mul(
                hb.ap()[:, 0:1024], t.ap()[:, 0:1024], p[:, :]),
                reads=[tres, PS[upi]], writes=[hres])
            P.dma("gpsimd", d_gated[h, :, th * 1024:(th + 1) * 1024],
                  hb.ap()[:, 0:1024], reads=[hres], writes=[DGATED])

    LAMF = P.barrier_res([LAMF, ROWS])
    STH = P.barrier_res([STF])
    ACH = P.barrier_res([ACF])

    def hidden_mm(wdram, m, rhs_fn, pool, pool_pi):
        for piece in range(4):
            slot = piece % 4
            slab = load_slab(wdram, m, piece * 16, 16, piece % 2, slot)
            fns = []
            for k2 in range(16):
                kc = piece * 16 + k2
                for s in range(2):
                    sl = slice(s * 512, (s + 1) * 512)
                    fns.append(lambda E, kc=kc, k2=k2, sl=sl, sb=slab,
                               p=pool: E.matmul(
                                   p[:, sl], sb[:, k2, :], rhs_fn(kc, sl),
                                   start=(kc == 0), stop=(kc == HC - 1)))
            P.op("tensor", fns, reads=[WSL[slot], STH, ACH],
                 writes=[PS[pool_pi]])

    def gated_rhs(kc, sl):
        area = st_h if kc < 32 else ac_h
        return area[:, kc % 32, sl]

    for th in range(2):
        for h in range(HC):
            area = st_h if h < 32 else ac_h
            P.dma("gpsimd", area[:, h % 32, :],
                  d_gated[h, :, th * 1024:(th + 1) * 1024],
                  reads=[DGATED], writes=[STH if h < 32 else ACH])
        for m in range(KC):
            pidx = m % 2
            pool = mm0 if pidx == 0 else mm1
            hidden_mm(din["wd"], m, gated_rhs, pool, pidx)
            off = pidx * 1024
            P.op("scalar", lambda E, p=pool, off=off: E.activation(
                sb_lam.ap()[:, off:off + 1024], p[:, :], AF.Silu),
                reads=[PS[pidx]], writes=[LAMF])
            P.dma("gpsimd", d_g[m, :, th * 1024:(th + 1) * 1024],
                  sb_lam.ap()[:, off:off + 1024], reads=[LAMF], writes=[DG])

    for th in range(2):
        for h in range(HC):
            area = st_h if h < 32 else ac_h
            P.dma("gpsimd", area[:, h % 32, :],
                  d_h2[h, :, th * 1024:(th + 1) * 1024],
                  reads=[DH2], writes=[STH if h < 32 else ACH])
        for m in range(KC):
            pidx = m % 2
            pool = mm0 if pidx == 0 else mm1
            hidden_mm(din["wfv"], m, gated_rhs, pool, pidx)
            a1t, ares = (sb_tmp, TMP) if pidx == 0 else (sb_tmp2, TMP2)
            x0b = x0b_ap[pidx]
            goff = pidx * 1024
            P.dma("gpsimd", a1t.ap()[:, 0:1024],
                  d_a1[m, :, th * 1024:(th + 1) * 1024],
                  reads=[DA1], writes=[ares])
            P.dma("gpsimd", sb_lam.ap()[:, goff:goff + 1024],
                  d_g[m, :, th * 1024:(th + 1) * 1024],
                  reads=[DG], writes=[LAMF])
            P.dma("gpsimd", x0b,
                  x0v[m, :, th * 1024:(th + 1) * 1024], writes=[ares])
            P.op("vector", lambda E, p=pool, a=a1t: E.tensor_add(
                a.ap()[:, 0:1024], p[:, :], a.ap()[:, 0:1024]),
                reads=[PS[pidx], ares], writes=[ares])
            P.op("vector", lambda E, a=a1t, goff=goff: E.tensor_mul(
                a.ap()[:, 0:1024], a.ap()[:, 0:1024],
                sb_lam.ap()[:, goff:goff + 1024]),
                reads=[ares, LAMF], writes=[ares])
            P.op("vector", lambda E, a=a1t, xb=x0b: E.tensor_add(
                a.ap()[:, 0:1024], a.ap()[:, 0:1024], xb),
                reads=[ares], writes=[ares])
            P.dma("sync", o_x[m, :, th * 1024:(th + 1) * 1024],
                  a1t.ap()[:, 0:1024], reads=[ares])

    if debug:
        P.dma("gpsimd", dbg["dbg_xn"], d_xn.flatten(), reads=[DXN])
    if debug2:
        P.dma("gpsimd", dbg["d2_ccin"], cc_in.ap(), reads=[CCIO])
        P.dma("gpsimd", dbg["d2_ccout"], cc_out.ap(), reads=[CCIO])
        P.dma("gpsimd", dbg["d2_C16"], sb_C16.ap(), reads=[SMALL])
        P.dma("gpsimd", dbg["d2_Ff"], sb_F.ap()[:, 0:64], reads=[FRES])
        P.dma("gpsimd", dbg["d2_wA"], sb_wA.ap()[:, 0:64], reads=[W4])
        P.dma("gpsimd", dbg["d2_wC"], sb_wC.ap()[:, 0:64], reads=[FRES])
        P.dma("gpsimd", dbg["d2_netd"], sm[:, S_NETD:S_NETD + 16], reads=[SMALL])

    P.stream["sync"].append((
        [lambda E: E.nop()],
        [(s, P.tot[s]) for s in P.tot if P.tot[s] > 0], None, 0))

    with nc.Block() as block:
        def run(E, name):
            for fns, waits, sem_, by in P.stream[name]:
                for s, v in waits:
                    E.wait_ge(sems[s], v)
                inst = None
                for fn in fns:
                    inst = fn(E)
                if sem_ is not None and by > 0:
                    inst.then_inc(sems[sem_], by)

        @block.sync
        def _(E):
            run(E, "sync")

        @block.tensor
        def _(E):
            run(E, "tensor")

        @block.vector
        def _(E):
            run(E, "vector")

        @block.scalar
        def _(E):
            run(E, "scalar")

        @block.gpsimd
        def _(E):
            run(E, "gpsimd")

    stack.close()
    return nc


def _tile_w(W):
    K, M = W.shape
    t = W.reshape(K // 128, 128, M // 128, 128)
    return np.ascontiguousarray(t.transpose(2, 1, 0, 3))


def _vec128(v):
    return np.ascontiguousarray(np.asarray(v, np.float32).reshape(16, 128).T)


def kernel(**inputs):
    f32 = np.float32
    x = np.asarray(inputs["L0_attn_x"], f32)
    td = np.asarray(inputs["L0_attn_time_decay"], f32).reshape(-1)
    tf = np.asarray(inputs["L0_attn_time_first"], f32).reshape(-1)
    shared = {
        "wk": _tile_w(np.asarray(inputs["L0_attn_Wk"], f32)),
        "wv": _tile_w(np.asarray(inputs["L0_attn_Wv"], f32)),
        "wr": _tile_w(np.asarray(inputs["L0_attn_Wr"], f32)),
        "wo": _tile_w(np.asarray(inputs["L0_attn_Wo"], f32)),
        "wfk": _tile_w(np.asarray(inputs["L0_attn_Wfk"], f32)),
        "wfv": _tile_w(np.asarray(inputs["L0_attn_Wfv"], f32)),
        "wg": _tile_w(np.ascontiguousarray(
            np.asarray(inputs["L0_ffn_w_gate"], f32).T)),
        "wu": _tile_w(np.ascontiguousarray(
            np.asarray(inputs["L0_ffn_w_up"], f32).T)),
        "wd": _tile_w(np.ascontiguousarray(
            np.asarray(inputs["L0_ffn_w_down"], f32).T)),
        "td": _vec128(td), "tf": _vec128(tf),
        "ln1w": _vec128(inputs["L0_attn_ln1_w"]),
        "ln1b": _vec128(inputs["L0_attn_ln1_b"]),
        "ln2w": _vec128(inputs["L0_attn_ln2_w"]),
        "ln2b": _vec128(inputs["L0_attn_ln2_b"]),
        "zd": np.ascontiguousarray(np.concatenate(
            [np.ones((128, 1)), np.zeros((128, TK - 1))], axis=1)
            .astype(ml_dtypes.bfloat16)),
        "ones_bf": np.ones((128, 1), ml_dtypes.bfloat16),
        "ones_f": np.ones((33, 128), f32),
    }

    in_maps = []
    for b in range(2):
        for q in range(4):
            x0c = np.ascontiguousarray(x[b, q * TK:(q + 1) * TK, :].T)
            epow = np.zeros((128, 4), f32)
            cmask = np.zeros((128, 4), f32)
            for p in range(q):
                epow[:, p] = 2048.0 * (q - 1 - p)
                cmask[:, p] = 1.0
            m = dict(shared)
            m["x0"] = x0c.reshape(KC, 128, TK)
            m["epow"] = epow
            cm512 = np.zeros((128, 512), f32)
            cm512[:, 0:64] = np.tile(cmask, (1, KC))
            m["cm512"] = cm512.astype(ml_dtypes.bfloat16)
            bm2 = np.zeros((128, 2), f32)
            bm2[:, 0] = 1.0 if q == 0 else 0.0
            m["bmask"] = bm2
            in_maps.append(m)

    nc = build_program(debug=DEBUG, debug2=DEBUG2)
    res = run_bass_kernel_spmd(nc, in_maps, core_ids=list(range(8)))
    out = np.empty((2, 8192, C), f32)
    for b in range(2):
        for q in range(4):
            r = np.asarray(res.results[b * 4 + q]["o_x"]).reshape(C, TK)
            out[b, q * TK:(q + 1) * TK, :] = r.T
    if DEBUG or DEBUG2:
        kernel.last_results = res
    return out


# version 06
# speedup vs baseline: 1.1153x; 1.1153x over previous
import numpy as np
import ml_dtypes
from contextlib import ExitStack

import concourse.bass as bass
import concourse.mybir as mybir
from concourse.bass_utils import run_bass_kernel_spmd

F32 = mybir.dt.float32
BF16 = mybir.dt.bfloat16
AF = mybir.ActivationFunctionType
ALU = mybir.AluOpType
AXX = mybir.AxisListType.X

C = 2048
TK = 2048
KC = 16
HC = 64
AR = 1024
EPS = 1e-5

DEBUG = False
DEBUG2 = False

ENG_SEM = {"tensor": "pe", "vector": "ve", "scalar": "se", "gpsimd": "ge"}
FIFO_ENGINES = ("tensor", "vector", "scalar")


class Res:
    __slots__ = ("w", "r")

    def __init__(self, preload=()):
        self.w = None
        self.r = list(preload)


class Prog:
    def __init__(self, nc, sems):
        self.nc = nc
        self.sems = sems
        self.tot = {s: 0 for s in sems}
        self.stream = {e: [] for e in
                       ["sync", "tensor", "vector", "scalar", "gpsimd"]}
        self.hw = {e: {} for e in self.stream}

    def _waits(self, eng, reads, writes):
        need = {}
        for r in reads:
            if r.w:
                need.setdefault(r.w[0], []).append(r.w)
        for w in writes:
            for ev in ([w.w] if w.w else []) + w.r:
                need.setdefault(ev[0], []).append(ev)
        out = []
        for s, evs in need.items():
            if (eng in FIFO_ENGINES and all(ev[2] == eng for ev in evs)):
                continue
            v = max(ev[1] for ev in evs)
            if self.hw[eng].get(s, 0) >= v:
                continue
            out.append((s, v))
        return out

    def op(self, eng, fns, reads=(), writes=(), sem=None, by=1, fifo=True):
        if callable(fns):
            fns = [fns]
        reads = list(reads)
        writes = list(writes)
        sem = sem or ENG_SEM[eng]
        waits = self._waits(eng, reads, writes)
        for s, v in waits:
            assert v <= self.tot[s], f"forward wait {s}:{v}>{self.tot[s]}"
            self.hw[eng][s] = max(self.hw[eng].get(s, 0), v)
        self.tot[sem] += by
        ev = (sem, self.tot[sem], eng if fifo else "*")
        self.stream[eng].append((fns, waits, sem, by))
        for r in reads:
            r.r.append(ev)
        for w in writes:
            w.w = ev
            w.r = []
        return ev

    def dma(self, q, out, in_, reads=(), writes=(), slow=False):
        sem = {"sync": "wq", "gpsimd": "gq", "scalar": "sq", "vector": "vq"}[q]
        if slow:
            fn = lambda E, o=out, i=in_: E.dma_start(
                out=o, in_=i, allow_slow_non_contiguous=True)
        else:
            fn = lambda E, o=out, i=in_: E.dma_start(out=o, in_=i)
        return self.op(q, fn, reads=reads, writes=writes, sem=sem, by=16)

    def barrier_res(self, *groups):
        evs = []
        for g in groups:
            for r in (g if isinstance(g, (list, tuple)) else [g]):
                if r.w:
                    evs.append(r.w)
                evs.extend(r.r)
        return Res(preload=evs)


def build_program(debug=False, debug2=False):
    nc = bass.Bass("TRN2", target_bir_lowering=False, debug=False)

    din = {}
    for nm, shp in [
        ("x0", [KC, 128, TK]),
        ("wk", [KC, 128, KC, 128]), ("wv", [KC, 128, KC, 128]),
        ("wr", [KC, 128, KC, 128]), ("wo", [KC, 128, KC, 128]),
        ("wfk", [HC, 128, KC, 128]), ("wfv", [KC, 128, HC, 128]),
        ("wg", [HC, 128, KC, 128]), ("wu", [HC, 128, KC, 128]),
        ("wd", [KC, 128, HC, 128]),
        ("td", [128, KC]), ("tf", [128, KC]),
        ("ln1w", [128, KC]), ("ln1b", [128, KC]),
        ("ln2w", [128, KC]), ("ln2b", [128, KC]),
        ("zd", [128, TK]),
        ("epow", [128, 4]), ("cm512", [128, 512]), ("bmask", [128, 2]),
        ("ones_bf", [128, 1]), ("ones_f", [33, 128]),
    ]:
        dt = BF16 if nm in ("ones_bf", "zd", "cm512") else F32
        din[nm] = nc.dram_tensor(nm, shp, dt, kind="ExternalInput").ap()

    o_x = nc.dram_tensor("o_x", [KC, 128, TK], F32, kind="ExternalOutput").ap()
    dbg = {}
    if debug2:
        for nm, shp, dt in [
            ("d2_ccin", [KC, 128], BF16),
            ("d2_ccout", [4, KC, 128], BF16),
            ("d2_C16", [128, KC], F32),
            ("d2_Ff", [128, 64], BF16),
            ("d2_wA", [128, 64], BF16),
            ("d2_wC", [128, 64], BF16),
            ("d2_netd", [128, KC], F32),
        ]:
            dbg[nm] = nc.dram_tensor(nm, shp, dt, kind="ExternalOutput").ap()
    if debug:
        for nm, shp, dt in [
            ("dbg_states", [128, KC * TK], BF16),
            ("dbg_rwkv", [128, KC * TK], BF16),
            ("dbg_xl2", [128, KC * TK], BF16),
            ("dbg_f", [128, KC], F32),
            ("dbg_C", [128, KC], F32),
            ("dbg_xn", [KC * 128 * TK], BF16),
            ("dbg_Ff", [128, 64], BF16),
            ("dbg_xl", [128, KC * TK], BF16),
        ]:
            dbg[nm] = nc.dram_tensor(nm, shp, dt, kind="ExternalOutput").ap()

    d_xn = nc.dram_tensor("d_xn", [KC, 128, TK], BF16).ap()
    d_a1 = nc.dram_tensor("d_a1", [KC, 128, TK], F32).ap()
    d_gated = nc.dram_tensor("d_gated", [HC, 128, TK], BF16).ap()
    d_h2 = nc.dram_tensor("d_h2", [HC, 128, TK], BF16).ap()
    d_g = nc.dram_tensor("d_g", [KC, 128, TK], F32).ap()
    cc_in = nc.dram_tensor("cc_in", [KC, 128], BF16)
    cc_out = nc.dram_tensor("cc_out", [4, KC, 128], BF16)

    stack = ExitStack()
    e = stack.enter_context

    sb_st = e(nc.sbuf_tensor("sb_st", [128, KC * TK], BF16))
    sb_ac = e(nc.sbuf_tensor("sb_ac", [128, KC * TK], BF16))
    st_v = sb_st.ap().rearrange("p (m t) -> p m t", m=KC)
    ac_v = sb_ac.ap().rearrange("p (m t) -> p m t", m=KC)
    st_h = sb_st.ap().rearrange("p (h t) -> p h t", h=32)
    ac_h = sb_ac.ap().rearrange("p (h t) -> p h t", h=32)

    sb_wst = e(nc.sbuf_tensor("sb_wst", [128, 4096], F32))
    sb_wbf = e(nc.sbuf_tensor("sb_wbf", [128, 8192], BF16))
    wst_v = sb_wst.ap().rearrange("p (b k) -> p b k", b=2)
    wbf_v = sb_wbf.ap().rearrange("p (b k) -> p b k", b=4)

    sb_tmp = e(nc.sbuf_tensor("sb_tmp", [128, TK], F32))
    sb_tmp2 = e(nc.sbuf_tensor("sb_tmp2", [128, TK], F32))
    sb_bft = e(nc.sbuf_tensor("sb_bft", [128, TK], BF16))
    sb_bft2 = e(nc.sbuf_tensor("sb_bft2", [128, TK], BF16))
    sb_lam = e(nc.sbuf_tensor("sb_lam", [128, TK], F32))
    u_ap = [sb_lam.ap()[:, 0:AR], sb_lam.ap()[:, AR:2 * AR]]
    x0b_ap = [sb_tmp.ap()[:, AR:2 * AR], sb_tmp2.ap()[:, AR:2 * AR]]

    sb_sm = e(nc.sbuf_tensor("sb_sm", [128, 80], F32))
    sm = sb_sm.ap()
    S_TD, S_TF, S_ETD, S_NETD, S_LAM, S_ETF, S_V0, S_CAR = (
        0, 16, 32, 48, 64, 80, 96, 112)
    S_C, S_T1 = 0, 16
    sb_F = e(nc.sbuf_tensor("sb_F", [128, 512], BF16))
    sb_sml = e(nc.sbuf_tensor("sb_sml", [128, 16], F32))
    sb_ep = sb_sml.ap()[:, 0:4]
    sb_cm = sb_sml.ap()[:, 4:8]
    sb_w4h = sb_sml.ap()[:, 8:12]
    sb_bm = sb_sml.ap()[:, 12:13]
    sb_zero = sb_sml.ap()[:, 13:14]
    sb_ln = e(nc.sbuf_tensor("sb_ln", [128, 4 * KC], F32))
    ln_v = sb_ln.ap().rearrange("p (a k) -> p a k", a=4)
    sb_o1 = e(nc.sbuf_tensor("sb_o1", [128, 1], BF16))
    sb_wA = e(nc.sbuf_tensor("sb_wA", [128, 512], BF16))
    sb_wC = e(nc.sbuf_tensor("sb_wC", [128, 512], BF16))
    sb_cm64 = e(nc.sbuf_tensor("sb_cm64", [128, 512], BF16))
    sb_zd = e(nc.sbuf_tensor("sb_zd", [128, TK], BF16))
    sb_C16 = e(nc.sbuf_tensor("sb_C16", [128, KC], F32))
    sb_etf16 = e(nc.sbuf_tensor("sb_etf16", [128, KC], F32))
    sb_v016 = e(nc.sbuf_tensor("sb_v016", [128, KC], F32))
    sb_boost = e(nc.sbuf_tensor("sb_boost", [128, KC], F32))
    sb_o1f = e(nc.sbuf_tensor("sb_o1f", [33, 128], F32))

    pst = [e(nc.psum_tensor(f"ps{i}", [128, 1024], F32)) for i in range(4)]
    POOLS = [p.ap() for p in pst]
    mm0, mm1, ax0, ax1 = POOLS

    sems = {s: e(nc.semaphore(s))
            for s in ["wq", "gq", "sq", "vq", "pe", "ve", "se", "ge", "cc"]}
    P = Prog(nc, sems)

    PS = [Res() for _ in range(4)]
    WST = [Res(), Res()]
    WSL = [Res() for _ in range(4)]
    ST = [Res() for _ in range(KC)]
    AXN = [Res() for _ in range(KC)]
    AXL = [Res() for _ in range(KC)]
    TMP, TMP2, BFT, BFT2, LAM, U, U2 = (Res() for _ in range(7))
    IOTA, ROWS, SMALL, W4, FRES = (Res() for _ in range(5))
    DXN, DXL, DA1, DGATED, DH2, DG, CCIO = (Res() for _ in range(7))

    cast_alt = [0]
    wq_alt = [0]

    def load_slab(wdram, mc, kc0, nkc, stage, slot, cast_eng=None):
        src = wdram[mc, :, kc0:kc0 + nkc, :]
        dst = wst_v[:, stage, 0:nkc * 128].rearrange("p (k f) -> p k f", k=nkc)
        wq_eng = "sync" if wq_alt[0] == 0 else "scalar"
        wq_alt[0] ^= 1
        P.dma(wq_eng, dst, src, writes=[WST[stage]])
        stg = wst_v[:, stage, 0:nkc * 128]
        bfdst = wbf_v[:, slot, 0:nkc * 128]
        if cast_eng is None:
            cast_eng = "gpsimd" if cast_alt[0] == 0 else "vector"
            cast_alt[0] ^= 1
        P.op(cast_eng, lambda E, o=bfdst, i=stg: E.tensor_copy(o, i),
             reads=[WST[stage]], writes=[WSL[slot]])
        return wbf_v[:, slot, :].rearrange("p (k f) -> p k f", k=KC)

    P.dma("gpsimd", sm[:, S_TD:S_TD + 16], din["td"], writes=[SMALL])
    P.dma("gpsimd", sm[:, S_TF:S_TF + 16], din["tf"], writes=[SMALL])
    P.dma("gpsimd", ln_v[:, 0, :], din["ln1w"], writes=[SMALL])
    P.dma("gpsimd", ln_v[:, 1, :], din["ln1b"], writes=[SMALL])
    P.dma("gpsimd", ln_v[:, 2, :], din["ln2w"], writes=[SMALL])
    P.dma("gpsimd", ln_v[:, 3, :], din["ln2b"], writes=[SMALL])
    P.dma("gpsimd", sb_zd.ap(), din["zd"], writes=[IOTA])
    P.dma("gpsimd", sb_ep, din["epow"], writes=[SMALL])
    P.dma("gpsimd", sb_cm64.ap(), din["cm512"], writes=[SMALL])
    P.dma("gpsimd", sb_sml.ap()[:, 12:14], din["bmask"], writes=[SMALL])
    P.dma("gpsimd", sb_o1.ap(), din["ones_bf"], writes=[SMALL])
    P.dma("gpsimd", sb_o1f.ap(), din["ones_f"], writes=[SMALL])

    P.op("scalar", lambda E: E.activation(
        sm[:, S_ETD:S_ETD + 16], sm[:, S_TD:S_TD + 16], AF.Exp),
        reads=[SMALL], writes=[SMALL])
    P.op("vector", lambda E: E.tensor_scalar_mul(
        sm[:, S_NETD:S_NETD + 16], sm[:, S_ETD:S_ETD + 16], -1.0),
        reads=[SMALL], writes=[SMALL])
    P.op("scalar", lambda E: E.activation(
        sm[:, S_LAM:S_LAM + 16], sm[:, S_NETD:S_NETD + 16], AF.Exp),
        reads=[SMALL], writes=[SMALL])
    P.op("scalar", lambda E: E.activation(
        sb_etf16.ap(), sm[:, S_TF:S_TF + 16], AF.Exp),
        reads=[SMALL], writes=[SMALL])
    P.op("vector", lambda E: E.tensor_scalar_mul(
        sb_etf16.ap(), sb_etf16.ap(), sb_bm),
        reads=[SMALL], writes=[SMALL])

    def rows_math(n_tok, sx_pis, sq_pis):
        mu = sb_tmp.ap()[0:1, 0:n_tok]
        va = sb_tmp2.ap()[0:1, 0:n_tok]
        sc = sb_lam.ap()[0:1, 0:n_tok]
        bi = sb_wst.ap()[0:1, 0:n_tok]
        for i, pi in enumerate(sx_pis):
            P.op("vector", lambda E, i=i, pi=pi, mu=mu: E.tensor_scalar_mul(
                mu[0:1, i * 1024:(i + 1) * 1024], POOLS[pi][0:1, :], 1.0 / C),
                reads=[PS[pi]], writes=[TMP])
        for i, pi in enumerate(sq_pis):
            P.op("vector", lambda E, i=i, pi=pi, va=va: E.tensor_scalar_mul(
                va[0:1, i * 1024:(i + 1) * 1024], POOLS[pi][0:1, :], 1.0 / C),
                reads=[PS[pi]], writes=[TMP2])
        P.op("vector", lambda E, sc=sc, mu=mu: E.tensor_mul(sc, mu, mu),
             reads=[TMP, ROWS], writes=[ROWS])
        P.op("vector", lambda E, va=va, sc=sc: E.tensor_sub(va, va, sc),
             reads=[TMP2, ROWS], writes=[TMP2])
        P.op("vector", lambda E, va=va: E.tensor_scalar_add(va, va, EPS),
             reads=[TMP2], writes=[TMP2])
        P.op("scalar", lambda E, va=va: E.activation(va, va, AF.Sqrt),
             reads=[TMP2], writes=[TMP2])
        P.op("vector", lambda E, sc=sc, va=va: E.reciprocal(sc, va),
             reads=[TMP2, ROWS], writes=[ROWS])
        P.op("vector", lambda E, bi=bi, mu=mu, sc=sc: E.tensor_mul(bi, mu, sc),
             reads=[TMP, ROWS, WST[0]], writes=[WST[0]])
        P.op("vector", lambda E, bi=bi: E.tensor_scalar_mul(bi, bi, -1.0),
             reads=[WST[0]], writes=[WST[0]])

    def bcast_rows(n_tok, sc_pi, bi_pi, off=0):
        sc = sb_lam.ap()[0:1, :]
        bi = sb_wst.ap()[0:1, :]
        for s in range(n_tok // 512):
            sl = slice(s * 512, (s + 1) * 512)
            gsl = slice(off + s * 512, off + (s + 1) * 512)
            P.op("tensor", lambda E, sl=sl, gsl=gsl, pi=sc_pi, sc=sc: E.matmul(
                POOLS[pi][:, sl], sb_o1f.ap()[0:1, :], sc[0:1, gsl],
                start=True, stop=True),
                reads=[ROWS, SMALL], writes=[PS[sc_pi]])
            P.op("tensor", lambda E, sl=sl, gsl=gsl, pi=bi_pi, bi=bi: E.matmul(
                POOLS[pi][:, sl], sb_o1f.ap()[0:1, :], bi[0:1, gsl],
                start=True, stop=True),
                reads=[WST[0], SMALL], writes=[PS[bi_pi]])

    x0v = din["x0"]
    for r in range(2):
        t0 = r * AR
        rsl = slice(t0, t0 + AR)
        ROWS = P.barrier_res([ROWS, U, U2])
        for kc in range(KC):
            P.dma("gpsimd", sb_tmp.ap()[:, 0:AR], x0v[kc, :, rsl],
                  writes=[TMP])
            P.op("vector", lambda E: E.tensor_copy(
                sb_bft.ap()[:, 0:AR], sb_tmp.ap()[:, 0:AR]),
                reads=[TMP], writes=[BFT])
            P.op("scalar", lambda E: E.activation(
                sb_bft2.ap()[:, 0:AR], sb_tmp.ap()[:, 0:AR], AF.Square),
                reads=[TMP], writes=[BFT2])
            for s in range(2):
                sl = slice(s * 512, (s + 1) * 512)
                P.op("tensor", lambda E, sl=sl, kc=kc: E.matmul(
                    mm0[0:1, sl], sb_o1.ap(), sb_bft.ap()[:, sl],
                    start=(kc == 0), stop=(kc == KC - 1)),
                    reads=[BFT, SMALL], writes=[PS[0]])
                P.op("tensor", lambda E, sl=sl, kc=kc: E.matmul(
                    ax0[0:1, sl], sb_o1.ap(), sb_bft2.ap()[:, sl],
                    start=(kc == 0), stop=(kc == KC - 1)),
                    reads=[BFT2, SMALL], writes=[PS[2]])
        rows_math(AR, [0], [2])
        bcast_rows(AR, 1, 3)

        for kc in range(KC):
            P.dma("gpsimd", sb_tmp.ap()[:, 0:AR], x0v[kc, :, rsl],
                  writes=[TMP])
            P.op("vector", lambda E: E.tensor_mul(
                sb_tmp2.ap()[:, 0:AR], sb_tmp.ap()[:, 0:AR], mm1[:, :]),
                reads=[TMP, PS[1]], writes=[TMP2])
            P.op("vector", lambda E, kc=kc, rsl=rsl: E.tensor_add(
                st_v[:, kc, rsl], sb_tmp2.ap()[:, 0:AR], ax1[:, :]),
                reads=[TMP2, PS[3]], writes=[AXN[kc]])
            P.dma("gpsimd", d_xn[kc, :, rsl], st_v[:, kc, rsl],
                  reads=[AXN[kc]], writes=[DXN])

        for kc in range(KC):
            P.op("scalar", lambda E, kc=kc, rsl=rsl: E.activation(
                sb_bft2.ap()[:, 0:AR], st_v[:, kc, rsl], AF.Square),
                reads=[AXN[kc]], writes=[BFT2])
            for s in range(2):
                sl = slice(s * 512, (s + 1) * 512)
                gsl = slice(t0 + s * 512, t0 + (s + 1) * 512)
                P.op("tensor", lambda E, sl=sl, kc=kc, gsl=gsl: E.matmul(
                    mm0[0:1, sl], sb_o1.ap(), st_v[:, kc, gsl],
                    start=(kc == 0), stop=(kc == KC - 1)),
                    reads=[AXN[kc], SMALL], writes=[PS[0]])
                P.op("tensor", lambda E, sl=sl, kc=kc: E.matmul(
                    ax0[0:1, sl], sb_o1.ap(), sb_bft2.ap()[:, sl],
                    start=(kc == 0), stop=(kc == KC - 1)),
                    reads=[BFT2, SMALL], writes=[PS[2]])
        rows_math(AR, [0], [2])
        bcast_rows(AR, 1, 3)

        for kc in range(KC):
            P.op("vector", lambda E, kc=kc, rsl=rsl: E.tensor_mul(
                sb_tmp2.ap()[:, 0:AR], st_v[:, kc, rsl], mm1[:, :]),
                reads=[AXN[kc], PS[1]], writes=[TMP2])
            P.op("vector", lambda E: E.tensor_add(
                sb_tmp2.ap()[:, 0:AR], sb_tmp2.ap()[:, 0:AR], ax1[:, :]),
                reads=[TMP2, PS[3]], writes=[TMP2])
            P.op("vector", lambda E, kc=kc, rsl=rsl: E.tensor_scalar(
                ac_v[:, kc, rsl], sb_tmp2.ap()[:, 0:AR],
                ln_v[:, 0, kc:kc + 1], ln_v[:, 1, kc:kc + 1],
                ALU.mult, ALU.add),
                reads=[TMP2, SMALL], writes=[AXL[kc]])

    U = P.barrier_res([U, U2, ROWS])
    for m in range(KC):
        kslab = load_slab(din["wk"], m, 0, KC, 0, 0)
        vslab = load_slab(din["wv"], m, 0, KC, 1, 1)
        ST[m] = P.barrier_res([ST[m], AXN[m]])
        for half in range(2):
            kpool, vpool = (mm0, mm1) if half == 0 else (ax0, ax1)
            kpi, vpi = (0, 1) if half == 0 else (2, 3)
            fns = []
            for kc in range(KC):
                for s in range(2):
                    gsl = slice(half * 1024 + s * 512,
                                half * 1024 + (s + 1) * 512)
                    sl = slice(s * 512, (s + 1) * 512)
                    fns.append(lambda E, kc=kc, sl=sl, gsl=gsl,
                               kp=kpool, ks=kslab: E.matmul(
                                   kp[:, sl], ks[:, kc, :], ac_v[:, kc, gsl],
                                   start=(kc == 0), stop=(kc == KC - 1)))
                for s in range(2):
                    gsl = slice(half * 1024 + s * 512,
                                half * 1024 + (s + 1) * 512)
                    sl = slice(s * 512, (s + 1) * 512)
                    fns.append(lambda E, kc=kc, sl=sl, gsl=gsl,
                               vp=vpool, vs=vslab: E.matmul(
                                   vp[:, sl], vs[:, kc, :], ac_v[:, kc, gsl],
                                   start=(kc == 0), stop=(kc == KC - 1)))
            P.op("tensor", fns, reads=[WSL[0], WSL[1]] + AXL,
                 writes=[PS[kpi], PS[vpi]])
            ktmp, kt_res = (sb_tmp, TMP) if half == 0 else (sb_tmp2, TMP2)
            P.op("scalar", lambda E, kp=kpool, kt=ktmp: E.activation(
                kt.ap()[:, 0:AR], kp[:, :], AF.Exp),
                reads=[PS[kpi]], writes=[kt_res])
            if half == 0:
                P.op("vector", lambda E, vp=vpool, m=m: E.tensor_copy(
                    sb_v016.ap()[:, m:m + 1], vp[:, 0:1]),
                    reads=[PS[vpi]], writes=[SMALL])
            P.op("vector", lambda E, vp=vpool, kt=ktmp, half=half:
                 E.tensor_mul(sb_lam.ap()[:, half * AR:(half + 1) * AR],
                              kt.ap()[:, 0:AR], vp[:, :]),
                 reads=[kt_res, PS[vpi]], writes=[U])
        P.op("vector", lambda E, m=m: E.tensor_tensor_scan(
            st_v[:, m, :],
            sm[:, S_LAM + m:S_LAM + m + 1].broadcast_to([128, TK]),
            sb_lam.ap()[:, 0:TK], 0.0, ALU.mult, ALU.add),
            reads=[U, SMALL], writes=[ST[m]], fifo=False)

    if debug:
        P.dma("gpsimd", dbg["dbg_states"], sb_st.ap(), reads=ST)

    P.dma("gpsimd", cc_in.ap().rearrange("m p -> p m"),
          st_v[:, :, TK - 1:TK].squeeze(), reads=ST, writes=[CCIO], slow=True)
    P.op("gpsimd", lambda E: E.collective_compute(
        "AllGather", ALU.bypass,
        replica_groups=[[0, 1, 2, 3], [4, 5, 6, 7]],
        ins=[cc_in.ap().opt()], outs=[cc_out.ap().opt()]),
        reads=[CCIO], writes=[CCIO], sem="cc")
    if debug:
        P.dma("gpsimd", dbg["dbg_f"], st_v[:, :, TK - 1:TK].squeeze(),
              reads=ST, slow=True)

    ACD = P.barrier_res(AXL)
    if debug:
        P.dma("gpsimd", dbg["dbg_xl"], sb_ac.ap(), reads=[ACD])

    def emit_F_block():
        Fv = sb_F.ap()[:, 0:64].rearrange("p (m r) -> p m r", r=4)
        for rk in range(4):
            P.dma("gpsimd", Fv[:, :, rk],
                  cc_out.ap()[rk].rearrange("m p -> p m"),
                  reads=[CCIO], writes=[FRES], slow=True)
        def spacer():
            P.op("vector", lambda E: E.tensor_copy(
                sb_tmp2.ap()[:, 0:TK], sb_tmp2.ap()[:, 0:TK]),
                reads=[TMP2], writes=[TMP2])
        if debug:
            P.dma("gpsimd", dbg["dbg_Ff"], sb_F.ap()[:, 0:64], reads=[FRES])
        for m in range(KC):
            P.op("scalar", lambda E, m=m: E.activation(
                sb_wA.ap()[:, 4 * m:4 * m + 4], sb_ep, AF.Exp,
                scale=sm[:, S_NETD + m:S_NETD + m + 1]),
                reads=[SMALL], writes=[W4])
        spacer()
        P.op("vector", lambda E: E.tensor_mul(
            sb_wC.ap(), sb_F.ap(), sb_cm64.ap()),
            reads=[FRES, SMALL], writes=[FRES])
        P.op("vector", lambda E: E.tensor_mul(sb_wC.ap(), sb_wC.ap(), sb_wA.ap()),
             reads=[FRES, W4], writes=[FRES])
        spacer()
        P.op("vector", lambda E: E.reduce_sum(
            sb_C16.ap(), sb_wC.ap()[:, 0:64].rearrange("p (m r) -> p m r", r=4),
            AXX),
            reads=[FRES], writes=[SMALL])
        P.op("vector", lambda E: E.tensor_mul(
            sb_boost.ap(), sb_etf16.ap(), sb_v016.ap()),
            reads=[SMALL], writes=[SMALL])
        spacer()
        spacer()
        if debug:
            P.dma("gpsimd", dbg["dbg_C"], sb_C16.ap(), reads=[SMALL])

    LAMF = P.barrier_res([LAM, U, U2, ROWS])

    def mm_full(slab, rhs_fn, par, rhs_res):
        pools = (mm0, mm1) if par == 0 else (ax0, ax1)
        pis = (0, 1) if par == 0 else (2, 3)
        fns = []
        for kc in range(KC):
            for s in range(4):
                pool = pools[s // 2]
                sl = slice((s % 2) * 512, (s % 2 + 1) * 512)
                fns.append(lambda E, kc=kc, s=s, pool=pool, sl=sl, sb=slab:
                           E.matmul(pool[:, sl], sb[:, kc, :], rhs_fn(kc, s),
                                    start=(kc == 0), stop=(kc == KC - 1)))
        return pools, pis, fns

    pend_corr = []
    for m in range(KC):
        if m == 2:
            emit_F_block()
            for f in pend_corr:
                f()
            pend_corr = []
        par = m % 2
        rslab = load_slab(din["wr"], m, 0, KC, par, par * 2, cast_eng="gpsimd")
        pools, pis, fns = mm_full(
            rslab, lambda kc, s: ac_v[:, kc, s * 512:(s + 1) * 512], par, ACD)
        P.op("tensor", fns, reads=[WSL[par * 2], ACD],
             writes=[PS[pis[0]], PS[pis[1]]])
        sig, sres = (sb_bft, BFT) if par == 0 else (sb_bft2, BFT2)
        P.op("scalar", lambda E, p=pools[0], sg=sig: E.activation(
            sg.ap()[:, 0:1024], p[:, :], AF.Sigmoid),
            reads=[PS[pis[0]]], writes=[sres])
        P.op("scalar", lambda E, p=pools[1], sg=sig: E.activation(
            sg.ap()[:, 1024:2048], p[:, :], AF.Sigmoid),
            reads=[PS[pis[1]]], writes=[sres])
        lbuf, lres = (sb_lam, LAMF) if par == 0 else (sb_tmp, TMP)
        P.op("vector", lambda E, m=m, lb=lbuf: E.tensor_tensor_scan(
            lb.ap(), sm[:, S_LAM + m:S_LAM + m + 1].broadcast_to([128, TK]),
            sb_zero.broadcast_to([128, TK]), 1.0, ALU.mult, ALU.add),
            reads=[SMALL], writes=[lres], fifo=False)

        def corr(m=m, lbuf=lbuf, lres=lres, sig=sig, sres=sres):
            P.op("vector", lambda E, m=m, lb=lbuf: E.scalar_tensor_tensor(
                st_v[:, m, :], lb.ap(), sb_C16.ap()[:, m:m + 1],
                sb_st.ap()[:, m * TK:(m + 1) * TK], ALU.mult, ALU.add),
                reads=[lres, SMALL, ST[m]], writes=[ST[m]])
            P.op("vector", lambda E, m=m: E.scalar_tensor_tensor(
                st_v[:, m, :], sb_zd.ap(), sb_boost.ap()[:, m:m + 1],
                sb_st.ap()[:, m * TK:(m + 1) * TK], ALU.mult, ALU.add),
                reads=[SMALL, ST[m]], writes=[ST[m]])
            P.op("vector", lambda E, m=m, sg=sig: E.tensor_mul(
                st_v[:, m, :], st_v[:, m, :], sg.ap()[:, 0:2048]),
                reads=[sres, ST[m]], writes=[ST[m]])

        if m < 2:
            pend_corr.append(corr)
        else:
            corr()
    if debug:
        P.dma("gpsimd", dbg["dbg_rwkv"], sb_st.ap(), reads=ST)

    for m in range(KC):
        par = m % 2
        oslab = load_slab(din["wo"], m, 0, KC, par, par * 2)
        pools, pis, fns = mm_full(
            oslab, lambda kc, s: st_v[:, kc, s * 512:(s + 1) * 512], par, None)
        P.op("tensor", fns, reads=[WSL[par * 2]] + ST,
             writes=[PS[pis[0]], PS[pis[1]]])
        xnb, xres = (sb_bft, BFT) if par == 0 else (sb_bft2, BFT2)
        P.dma("gpsimd", xnb.ap()[:, 0:2048], d_xn[m, :, :],
              reads=[DXN], writes=[xres])
        a1t, ares = (sb_tmp, TMP) if par == 0 else (sb_tmp2, TMP2)
        P.op("vector", lambda E, p=pools[0], a=a1t, xb=xnb: E.tensor_add(
            a.ap()[:, 0:1024], p[:, :], xb.ap()[:, 0:1024]),
            reads=[PS[pis[0]], xres], writes=[ares])
        P.op("vector", lambda E, p=pools[1], a=a1t, xb=xnb: E.tensor_add(
            a.ap()[:, 1024:2048], p[:, :], xb.ap()[:, 1024:2048]),
            reads=[PS[pis[1]], xres], writes=[ares])
        P.dma("gpsimd", d_a1[m, :, :], a1t.ap()[:, 0:2048],
              reads=[ares], writes=[DA1])

    STF = P.barrier_res(ST)
    for kc in range(KC):
        P.dma("gpsimd", sb_tmp.ap()[:, 0:2048], d_a1[kc, :, :],
              reads=[DA1], writes=[TMP])
        P.op("vector", lambda E, kc=kc: E.tensor_copy(
            st_v[:, kc, :], sb_tmp.ap()[:, 0:2048]),
            reads=[TMP], writes=[STF])
        P.op("scalar", lambda E: E.activation(
            sb_bft2.ap()[:, 0:2048], sb_tmp.ap()[:, 0:2048], AF.Square),
            reads=[TMP], writes=[BFT2])
        for s in range(4):
            pxi = s // 2
            sl = slice((s % 2) * 512, (s % 2 + 1) * 512)
            tsl = slice(s * 512, (s + 1) * 512)
            P.op("tensor", lambda E, pxi=pxi, sl=sl, kc=kc, tsl=tsl: E.matmul(
                POOLS[pxi][0:1, sl], sb_o1.ap(), st_v[:, kc, tsl],
                start=(kc == 0), stop=(kc == KC - 1)),
                reads=[STF, SMALL], writes=[PS[pxi]])
            P.op("tensor", lambda E, pxi=pxi, sl=sl, kc=kc, tsl=tsl: E.matmul(
                POOLS[2 + pxi][0:1, sl], sb_o1.ap(), sb_bft2.ap()[:, tsl],
                start=(kc == 0), stop=(kc == KC - 1)),
                reads=[BFT2, SMALL], writes=[PS[2 + pxi]])
    ROWS = P.barrier_res([ROWS, LAMF])
    rows_math(2048, [0, 1], [2, 3])
    for th in range(2):
        tsl = slice(th * 1024, (th + 1) * 1024)
        bcast_rows(1024, 0, 2, off=th * 1024)
        for kc in range(KC):
            P.op("vector", lambda E, kc=kc, tsl=tsl: E.tensor_mul(
                sb_tmp.ap()[:, 0:1024], st_v[:, kc, tsl], mm0[:, :]),
                reads=[STF, PS[0]], writes=[TMP])
            P.op("vector", lambda E: E.tensor_add(
                sb_tmp.ap()[:, 0:1024], sb_tmp.ap()[:, 0:1024], ax0[:, :]),
                reads=[TMP, PS[2]], writes=[TMP])
            P.op("vector", lambda E, kc=kc, tsl=tsl: E.tensor_scalar(
                st_v[:, kc, tsl], sb_tmp.ap()[:, 0:1024],
                ln_v[:, 2, kc:kc + 1], ln_v[:, 3, kc:kc + 1],
                ALU.mult, ALU.add),
                reads=[TMP, SMALL], writes=[STF])
    if debug:
        P.dma("gpsimd", dbg["dbg_xl2"], sb_st.ap(), reads=[STF])

    for h in range(HC):
        par = h % 2
        fslab = load_slab(din["wfk"], h, 0, KC, par, par * 2)
        pools, pis, fns = mm_full(
            fslab, lambda kc, s: st_v[:, kc, s * 512:(s + 1) * 512], par, None)
        P.op("tensor", fns, reads=[WSL[par * 2], STF],
             writes=[PS[pis[0]], PS[pis[1]]])
        tmp, tres = (sb_tmp, TMP) if par == 0 else (sb_tmp2, TMP2)
        hb, hres = (sb_bft, BFT) if par == 0 else (sb_bft2, BFT2)
        P.op("scalar", lambda E, p=pools[0], t=tmp: E.activation(
            t.ap()[:, 0:1024], p[:, :], AF.Relu),
            reads=[PS[pis[0]]], writes=[tres])
        P.op("scalar", lambda E, p=pools[1], t=tmp: E.activation(
            t.ap()[:, 1024:2048], p[:, :], AF.Relu),
            reads=[PS[pis[1]]], writes=[tres])
        P.op("vector", lambda E, t=tmp, hb=hb: E.tensor_mul(
            hb.ap()[:, 0:2048], t.ap()[:, 0:2048], t.ap()[:, 0:2048]),
            reads=[tres], writes=[hres])
        P.dma("gpsimd", d_h2[h, :, :], hb.ap()[:, 0:2048],
              reads=[hres], writes=[DH2])

    ACF = P.barrier_res([ACD])
    for kc in range(KC):
        P.dma(("gpsimd", "scalar")[kc % 2], ac_v[:, kc, :], d_xn[kc, :, :],
              reads=[DXN], writes=[ACF])
    for h in range(HC):
        par = h % 2
        gslab = load_slab(din["wg"], h, 0, KC, par, par * 2 + 0)
        uslab = load_slab(din["wu"], h, 0, KC, par, par * 2 + 1)
        for th in range(2):
            pidx = (h * 2 + th) % 2
            gpool, upool = (mm0, ax0) if pidx == 0 else (mm1, ax1)
            gpi, upi = (0, 2) if pidx == 0 else (1, 3)
            fns = []
            for kc in range(KC):
                for s in range(2):
                    sl = slice(s * 512, (s + 1) * 512)
                    rsl = slice(th * 1024 + s * 512, th * 1024 + (s + 1) * 512)
                    fns.append(lambda E, kc=kc, sl=sl, rsl=rsl, gp=gpool,
                               gs=gslab: E.matmul(
                                   gp[:, sl], gs[:, kc, :], ac_v[:, kc, rsl],
                                   start=(kc == 0), stop=(kc == KC - 1)))
                for s in range(2):
                    sl = slice(s * 512, (s + 1) * 512)
                    rsl = slice(th * 1024 + s * 512, th * 1024 + (s + 1) * 512)
                    fns.append(lambda E, kc=kc, sl=sl, rsl=rsl, up=upool,
                               us=uslab: E.matmul(
                                   up[:, sl], us[:, kc, :], ac_v[:, kc, rsl],
                                   start=(kc == 0), stop=(kc == KC - 1)))
            P.op("tensor", fns,
                 reads=[WSL[par * 2], WSL[par * 2 + 1], ACF],
                 writes=[PS[gpi], PS[upi]])
            tmp, tres = (sb_tmp, TMP) if pidx == 0 else (sb_tmp2, TMP2)
            hb, hres = (sb_bft, BFT) if pidx == 0 else (sb_bft2, BFT2)
            P.op("scalar", lambda E, p=gpool, t=tmp: E.activation(
                t.ap()[:, 0:1024], p[:, :], AF.Silu),
                reads=[PS[gpi]], writes=[tres])
            P.op("vector", lambda E, p=upool, t=tmp, hb=hb: E.tensor_mul(
                hb.ap()[:, 0:1024], t.ap()[:, 0:1024], p[:, :]),
                reads=[tres, PS[upi]], writes=[hres])
            P.dma("gpsimd", d_gated[h, :, th * 1024:(th + 1) * 1024],
                  hb.ap()[:, 0:1024], reads=[hres], writes=[DGATED])

    LAMF = P.barrier_res([LAMF, ROWS])
    STH = P.barrier_res([STF])
    ACH = P.barrier_res([ACF])

    def hidden_mm(wdram, m, rhs_fn, pool, pool_pi):
        for piece in range(4):
            slot = piece % 4
            slab = load_slab(wdram, m, piece * 16, 16, piece % 2, slot)
            fns = []
            for k2 in range(16):
                kc = piece * 16 + k2
                for s in range(2):
                    sl = slice(s * 512, (s + 1) * 512)
                    fns.append(lambda E, kc=kc, k2=k2, sl=sl, sb=slab,
                               p=pool: E.matmul(
                                   p[:, sl], sb[:, k2, :], rhs_fn(kc, sl),
                                   start=(kc == 0), stop=(kc == HC - 1)))
            P.op("tensor", fns, reads=[WSL[slot], STH, ACH],
                 writes=[PS[pool_pi]])

    def gated_rhs(kc, sl):
        area = st_h if kc < 32 else ac_h
        return area[:, kc % 32, sl]

    TILE_QS = ["gpsimd", "sync", "scalar"]
    for th in range(2):
        for h in range(HC):
            area = st_h if h < 32 else ac_h
            P.dma(TILE_QS[h % 3], area[:, h % 32, :],
                  d_gated[h, :, th * 1024:(th + 1) * 1024],
                  reads=[DGATED], writes=[STH if h < 32 else ACH])
        for m in range(KC):
            pidx = m % 2
            pool = mm0 if pidx == 0 else mm1
            hidden_mm(din["wd"], m, gated_rhs, pool, pidx)
            off = pidx * 1024
            P.op("scalar", lambda E, p=pool, off=off: E.activation(
                sb_lam.ap()[:, off:off + 1024], p[:, :], AF.Silu),
                reads=[PS[pidx]], writes=[LAMF])
            P.dma("gpsimd", d_g[m, :, th * 1024:(th + 1) * 1024],
                  sb_lam.ap()[:, off:off + 1024], reads=[LAMF], writes=[DG])

    for th in range(2):
        for h in range(HC):
            area = st_h if h < 32 else ac_h
            P.dma(TILE_QS[h % 3], area[:, h % 32, :],
                  d_h2[h, :, th * 1024:(th + 1) * 1024],
                  reads=[DH2], writes=[STH if h < 32 else ACH])
        for m in range(KC):
            pidx = m % 2
            pool = mm0 if pidx == 0 else mm1
            hidden_mm(din["wfv"], m, gated_rhs, pool, pidx)
            a1t, ares = (sb_tmp, TMP) if pidx == 0 else (sb_tmp2, TMP2)
            x0b = x0b_ap[pidx]
            goff = pidx * 1024
            P.dma("gpsimd", a1t.ap()[:, 0:1024],
                  d_a1[m, :, th * 1024:(th + 1) * 1024],
                  reads=[DA1], writes=[ares])
            P.dma("scalar", sb_lam.ap()[:, goff:goff + 1024],
                  d_g[m, :, th * 1024:(th + 1) * 1024],
                  reads=[DG], writes=[LAMF])
            P.dma("sync", x0b,
                  x0v[m, :, th * 1024:(th + 1) * 1024], writes=[ares])
            P.op("vector", lambda E, p=pool, a=a1t: E.tensor_add(
                a.ap()[:, 0:1024], p[:, :], a.ap()[:, 0:1024]),
                reads=[PS[pidx], ares], writes=[ares])
            P.op("vector", lambda E, a=a1t, goff=goff: E.tensor_mul(
                a.ap()[:, 0:1024], a.ap()[:, 0:1024],
                sb_lam.ap()[:, goff:goff + 1024]),
                reads=[ares, LAMF], writes=[ares])
            P.op("vector", lambda E, a=a1t, xb=x0b: E.tensor_add(
                a.ap()[:, 0:1024], a.ap()[:, 0:1024], xb),
                reads=[ares], writes=[ares])
            P.dma("sync", o_x[m, :, th * 1024:(th + 1) * 1024],
                  a1t.ap()[:, 0:1024], reads=[ares])

    if debug:
        P.dma("gpsimd", dbg["dbg_xn"], d_xn.flatten(), reads=[DXN])
    if debug2:
        P.dma("gpsimd", dbg["d2_ccin"], cc_in.ap(), reads=[CCIO])
        P.dma("gpsimd", dbg["d2_ccout"], cc_out.ap(), reads=[CCIO])
        P.dma("gpsimd", dbg["d2_C16"], sb_C16.ap(), reads=[SMALL])
        P.dma("gpsimd", dbg["d2_Ff"], sb_F.ap()[:, 0:64], reads=[FRES])
        P.dma("gpsimd", dbg["d2_wA"], sb_wA.ap()[:, 0:64], reads=[W4])
        P.dma("gpsimd", dbg["d2_wC"], sb_wC.ap()[:, 0:64], reads=[FRES])
        P.dma("gpsimd", dbg["d2_netd"], sm[:, S_NETD:S_NETD + 16], reads=[SMALL])

    P.stream["sync"].append((
        [lambda E: E.nop()],
        [(s, P.tot[s]) for s in P.tot if P.tot[s] > 0], None, 0))

    with nc.Block() as block:
        def run(E, name):
            for fns, waits, sem_, by in P.stream[name]:
                for s, v in waits:
                    E.wait_ge(sems[s], v)
                inst = None
                for fn in fns:
                    inst = fn(E)
                if sem_ is not None and by > 0:
                    inst.then_inc(sems[sem_], by)

        @block.sync
        def _(E):
            run(E, "sync")

        @block.tensor
        def _(E):
            run(E, "tensor")

        @block.vector
        def _(E):
            run(E, "vector")

        @block.scalar
        def _(E):
            run(E, "scalar")

        @block.gpsimd
        def _(E):
            run(E, "gpsimd")

    stack.close()
    return nc


def _tile_w(W):
    K, M = W.shape
    t = W.reshape(K // 128, 128, M // 128, 128)
    return np.ascontiguousarray(t.transpose(2, 1, 0, 3))


def _vec128(v):
    return np.ascontiguousarray(np.asarray(v, np.float32).reshape(16, 128).T)


def kernel(**inputs):
    f32 = np.float32
    x = np.asarray(inputs["L0_attn_x"], f32)
    td = np.asarray(inputs["L0_attn_time_decay"], f32).reshape(-1)
    tf = np.asarray(inputs["L0_attn_time_first"], f32).reshape(-1)
    shared = {
        "wk": _tile_w(np.asarray(inputs["L0_attn_Wk"], f32)),
        "wv": _tile_w(np.asarray(inputs["L0_attn_Wv"], f32)),
        "wr": _tile_w(np.asarray(inputs["L0_attn_Wr"], f32)),
        "wo": _tile_w(np.asarray(inputs["L0_attn_Wo"], f32)),
        "wfk": _tile_w(np.asarray(inputs["L0_attn_Wfk"], f32)),
        "wfv": _tile_w(np.asarray(inputs["L0_attn_Wfv"], f32)),
        "wg": _tile_w(np.ascontiguousarray(
            np.asarray(inputs["L0_ffn_w_gate"], f32).T)),
        "wu": _tile_w(np.ascontiguousarray(
            np.asarray(inputs["L0_ffn_w_up"], f32).T)),
        "wd": _tile_w(np.ascontiguousarray(
            np.asarray(inputs["L0_ffn_w_down"], f32).T)),
        "td": _vec128(td), "tf": _vec128(tf),
        "ln1w": _vec128(inputs["L0_attn_ln1_w"]),
        "ln1b": _vec128(inputs["L0_attn_ln1_b"]),
        "ln2w": _vec128(inputs["L0_attn_ln2_w"]),
        "ln2b": _vec128(inputs["L0_attn_ln2_b"]),
        "zd": np.ascontiguousarray(np.concatenate(
            [np.ones((128, 1)), np.zeros((128, TK - 1))], axis=1)
            .astype(ml_dtypes.bfloat16)),
        "ones_bf": np.ones((128, 1), ml_dtypes.bfloat16),
        "ones_f": np.ones((33, 128), f32),
    }

    in_maps = []
    for b in range(2):
        for q in range(4):
            x0c = np.ascontiguousarray(x[b, q * TK:(q + 1) * TK, :].T)
            epow = np.zeros((128, 4), f32)
            cmask = np.zeros((128, 4), f32)
            for p in range(q):
                epow[:, p] = 2048.0 * (q - 1 - p)
                cmask[:, p] = 1.0
            m = dict(shared)
            m["x0"] = x0c.reshape(KC, 128, TK)
            m["epow"] = epow
            cm512 = np.zeros((128, 512), f32)
            cm512[:, 0:64] = np.tile(cmask, (1, KC))
            m["cm512"] = cm512.astype(ml_dtypes.bfloat16)
            bm2 = np.zeros((128, 2), f32)
            bm2[:, 0] = 1.0 if q == 0 else 0.0
            m["bmask"] = bm2
            in_maps.append(m)

    nc = build_program(debug=DEBUG, debug2=DEBUG2)
    res = run_bass_kernel_spmd(nc, in_maps, core_ids=list(range(8)))
    out = np.empty((2, 8192, C), f32)
    for b in range(2):
        for q in range(4):
            r = np.asarray(res.results[b * 4 + q]["o_x"]).reshape(C, TK)
            out[b, q * TK:(q + 1) * TK, :] = r.T
    if DEBUG or DEBUG2:
        kernel.last_results = res
    return out


# version 10 (final)
# speedup vs baseline: 1.1353x; 1.0179x over previous
import numpy as np
import ml_dtypes
from contextlib import ExitStack

import concourse.bass as bass
import concourse.mybir as mybir
from concourse.bass_utils import run_bass_kernel_spmd

F32 = mybir.dt.float32
BF16 = mybir.dt.bfloat16
AF = mybir.ActivationFunctionType
ALU = mybir.AluOpType
AXX = mybir.AxisListType.X

C = 2048
TK = 2048
KC = 16
HC = 64
AR = 1024
EPS = 1e-5

DEBUG = False
DEBUG2 = False

ENG_SEM = {"tensor": "pe", "vector": "ve", "scalar": "se", "gpsimd": "ge"}
FIFO_ENGINES = ("tensor", "vector", "scalar")


class Res:
    __slots__ = ("w", "r")

    def __init__(self, preload=()):
        self.w = None
        self.r = list(preload)


class Prog:
    def __init__(self, nc, sems):
        self.nc = nc
        self.sems = sems
        self.tot = {s: 0 for s in sems}
        self.stream = {e: [] for e in
                       ["sync", "tensor", "vector", "scalar", "gpsimd"]}
        self.hw = {e: {} for e in self.stream}

    def _waits(self, eng, reads, writes):
        need = {}
        for r in reads:
            if r.w:
                need.setdefault(r.w[0], []).append(r.w)
        for w in writes:
            for ev in ([w.w] if w.w else []) + w.r:
                need.setdefault(ev[0], []).append(ev)
        out = []
        for s, evs in need.items():
            if (eng in FIFO_ENGINES and all(ev[2] == eng for ev in evs)):
                continue
            v = max(ev[1] for ev in evs)
            if self.hw[eng].get(s, 0) >= v:
                continue
            out.append((s, v))
        return out

    def op(self, eng, fns, reads=(), writes=(), sem=None, by=1, fifo=True):
        if callable(fns):
            fns = [fns]
        reads = list(reads)
        writes = list(writes)
        sem = sem or ENG_SEM[eng]
        waits = self._waits(eng, reads, writes)
        for s, v in waits:
            assert v <= self.tot[s], f"forward wait {s}:{v}>{self.tot[s]}"
            self.hw[eng][s] = max(self.hw[eng].get(s, 0), v)
        self.tot[sem] += by
        ev = (sem, self.tot[sem], eng if fifo else "*")
        self.stream[eng].append((fns, waits, sem, by))
        for r in reads:
            r.r.append(ev)
        for w in writes:
            w.w = ev
            w.r = []
        return ev

    def dma(self, q, out, in_, reads=(), writes=(), slow=False):
        sem = {"sync": "wq", "gpsimd": "gq", "scalar": "sq", "vector": "vq"}[q]
        if slow:
            fn = lambda E, o=out, i=in_: E.dma_start(
                out=o, in_=i, allow_slow_non_contiguous=True)
        else:
            fn = lambda E, o=out, i=in_: E.dma_start(out=o, in_=i)
        return self.op(q, fn, reads=reads, writes=writes, sem=sem, by=16)

    def barrier_res(self, *groups):
        evs = []
        for g in groups:
            for r in (g if isinstance(g, (list, tuple)) else [g]):
                if r.w:
                    evs.append(r.w)
                evs.extend(r.r)
        return Res(preload=evs)


def build_program(debug=False, debug2=False):
    nc = bass.Bass("TRN2", target_bir_lowering=False, debug=False)

    din = {}
    for nm, shp in [
        ("x0", [KC, 128, TK]),
        ("wk", [KC, 128, KC, 128]), ("wv", [KC, 128, KC, 128]),
        ("wr", [KC, 128, KC, 128]), ("wo", [KC, 128, KC, 128]),
        ("wfk", [HC, 128, KC, 128]), ("wfv", [KC, 128, HC, 128]),
        ("wg", [HC, 128, KC, 128]), ("wu", [HC, 128, KC, 128]),
        ("wd", [KC, 128, HC, 128]),
        ("td", [128, KC]), ("tf", [128, KC]),
        ("ln1w", [128, KC]), ("ln1b", [128, KC]),
        ("ln2w", [128, KC]), ("ln2b", [128, KC]),
        ("zd", [128, TK]),
        ("epow", [128, 4]), ("cm512", [128, 512]), ("bmask", [128, 2]),
        ("ones_bf", [128, 1]), ("ones_f", [33, 128]),
    ]:
        dt = BF16 if nm in ("ones_bf", "zd", "cm512") else F32
        din[nm] = nc.dram_tensor(nm, shp, dt, kind="ExternalInput").ap()

    o_x = nc.dram_tensor("o_x", [KC, 128, TK], F32, kind="ExternalOutput").ap()
    dbg = {}
    if debug2:
        for nm, shp, dt in [
            ("d2_ccin", [KC, 128], BF16),
            ("d2_ccout", [4, KC, 128], BF16),
            ("d2_C16", [128, KC], F32),
            ("d2_Ff", [128, 64], BF16),
            ("d2_wA", [128, 64], BF16),
            ("d2_wC", [128, 64], BF16),
            ("d2_netd", [128, KC], F32),
        ]:
            dbg[nm] = nc.dram_tensor(nm, shp, dt, kind="ExternalOutput").ap()
    if debug:
        for nm, shp, dt in [
            ("dbg_states", [128, KC * TK], BF16),
            ("dbg_rwkv", [128, KC * TK], BF16),
            ("dbg_xl2", [128, KC * TK], BF16),
            ("dbg_f", [128, KC], F32),
            ("dbg_C", [128, KC], F32),
            ("dbg_xn", [KC * 128 * TK], BF16),
            ("dbg_Ff", [128, 64], BF16),
            ("dbg_xl", [128, KC * TK], BF16),
        ]:
            dbg[nm] = nc.dram_tensor(nm, shp, dt, kind="ExternalOutput").ap()

    d_xn = nc.dram_tensor("d_xn", [KC, 128, TK], BF16).ap()
    d_a1 = nc.dram_tensor("d_a1", [KC, 128, TK], F32).ap()
    d_gated = nc.dram_tensor("d_gated", [HC, 128, TK], BF16).ap()
    d_h2 = nc.dram_tensor("d_h2", [HC, 128, TK], BF16).ap()
    d_g = nc.dram_tensor("d_g", [KC, 128, TK], F32).ap()
    cc_in = nc.dram_tensor("cc_in", [KC, 128], BF16)
    cc_out = nc.dram_tensor("cc_out", [4, KC, 128], BF16)

    stack = ExitStack()
    e = stack.enter_context

    sb_st = e(nc.sbuf_tensor("sb_st", [128, KC * TK], BF16))
    sb_ac = e(nc.sbuf_tensor("sb_ac", [128, KC * TK], BF16))
    st_v = sb_st.ap().rearrange("p (m t) -> p m t", m=KC)
    ac_v = sb_ac.ap().rearrange("p (m t) -> p m t", m=KC)
    st_h = sb_st.ap().rearrange("p (h t) -> p h t", h=32)
    ac_h = sb_ac.ap().rearrange("p (h t) -> p h t", h=32)

    sb_wst = e(nc.sbuf_tensor("sb_wst", [128, 4096], F32))
    sb_wbf = e(nc.sbuf_tensor("sb_wbf", [128, 8192], BF16))
    wst_v = sb_wst.ap().rearrange("p (b k) -> p b k", b=2)
    wbf_v = sb_wbf.ap().rearrange("p (b k) -> p b k", b=4)

    sb_tmp = e(nc.sbuf_tensor("sb_tmp", [128, TK], F32))
    sb_tmp2 = e(nc.sbuf_tensor("sb_tmp2", [128, TK], F32))
    sb_bft = e(nc.sbuf_tensor("sb_bft", [128, TK], BF16))
    sb_bft2 = e(nc.sbuf_tensor("sb_bft2", [128, TK], BF16))
    sb_lam = e(nc.sbuf_tensor("sb_lam", [128, TK], F32))
    u_ap = [sb_lam.ap()[:, 0:AR], sb_lam.ap()[:, AR:2 * AR]]
    x0b_ap = [sb_tmp.ap()[:, AR:2 * AR], sb_tmp2.ap()[:, AR:2 * AR]]

    sb_sm = e(nc.sbuf_tensor("sb_sm", [128, 80], F32))
    sm = sb_sm.ap()
    S_TD, S_TF, S_ETD, S_NETD, S_LAM, S_ETF, S_V0, S_CAR = (
        0, 16, 32, 48, 64, 80, 96, 112)
    S_C, S_T1 = 0, 16
    sb_F = e(nc.sbuf_tensor("sb_F", [128, 512], BF16))
    sb_sml = e(nc.sbuf_tensor("sb_sml", [128, 16], F32))
    sb_ep = sb_sml.ap()[:, 0:4]
    sb_cm = sb_sml.ap()[:, 4:8]
    sb_w4h = sb_sml.ap()[:, 8:12]
    sb_bm = sb_sml.ap()[:, 12:13]
    sb_zero = sb_sml.ap()[:, 13:14]
    sb_ln = e(nc.sbuf_tensor("sb_ln", [128, 4 * KC], F32))
    ln_v = sb_ln.ap().rearrange("p (a k) -> p a k", a=4)
    sb_o1 = e(nc.sbuf_tensor("sb_o1", [128, 1], BF16))
    sb_wA = e(nc.sbuf_tensor("sb_wA", [128, 512], BF16))
    sb_wC = e(nc.sbuf_tensor("sb_wC", [128, 512], BF16))
    sb_cm64 = e(nc.sbuf_tensor("sb_cm64", [128, 512], BF16))
    sb_zd = e(nc.sbuf_tensor("sb_zd", [128, TK], BF16))
    sb_C16 = e(nc.sbuf_tensor("sb_C16", [128, KC], F32))
    sb_etf16 = e(nc.sbuf_tensor("sb_etf16", [128, KC], F32))
    sb_v016 = e(nc.sbuf_tensor("sb_v016", [128, KC], F32))
    sb_boost = e(nc.sbuf_tensor("sb_boost", [128, KC], F32))
    sb_o1f = e(nc.sbuf_tensor("sb_o1f", [33, 128], F32))

    pst = [e(nc.psum_tensor(f"ps{i}", [128, 1024], F32)) for i in range(4)]
    POOLS = [p.ap() for p in pst]
    mm0, mm1, ax0, ax1 = POOLS

    sems = {s: e(nc.semaphore(s))
            for s in ["wq", "gq", "sq", "vq", "pe", "ve", "se", "ge", "cc"]}
    P = Prog(nc, sems)

    PS = [Res() for _ in range(4)]
    WST = [Res(), Res()]
    WSL = [Res() for _ in range(4)]
    ST = [Res() for _ in range(KC)]
    AXN = [Res() for _ in range(KC)]
    AXL = [Res() for _ in range(KC)]
    TMP, TMP2, BFT, BFT2, LAM, U, U2 = (Res() for _ in range(7))
    IOTA, ROWS, SMALL, W4, FRES = (Res() for _ in range(5))
    DXN, DXL, DA1, DGATED, DH2, DG, CCIO = (Res() for _ in range(7))

    cast_alt = [0]
    wq_alt = [0]

    def load_slab(wdram, mc, kc0, nkc, stage, slot, cast_eng=None):
        src = wdram[mc, :, kc0:kc0 + nkc, :]
        dst = wst_v[:, stage, 0:nkc * 128].rearrange("p (k f) -> p k f", k=nkc)
        wq_eng = "sync" if wq_alt[0] == 0 else "scalar"
        wq_alt[0] ^= 1
        P.dma(wq_eng, dst, src, writes=[WST[stage]])
        stg = wst_v[:, stage, 0:nkc * 128]
        bfdst = wbf_v[:, slot, 0:nkc * 128]
        if cast_eng is None:
            cast_eng = "gpsimd" if cast_alt[0] == 0 else "vector"
            cast_alt[0] ^= 1
        P.op(cast_eng, lambda E, o=bfdst, i=stg: E.tensor_copy(o, i),
             reads=[WST[stage]], writes=[WSL[slot]])
        return wbf_v[:, slot, :].rearrange("p (k f) -> p k f", k=KC)

    P.dma("gpsimd", sm[:, S_TD:S_TD + 16], din["td"], writes=[SMALL])
    P.dma("gpsimd", sm[:, S_TF:S_TF + 16], din["tf"], writes=[SMALL])
    P.dma("gpsimd", ln_v[:, 0, :], din["ln1w"], writes=[SMALL])
    P.dma("gpsimd", ln_v[:, 1, :], din["ln1b"], writes=[SMALL])
    P.dma("gpsimd", ln_v[:, 2, :], din["ln2w"], writes=[SMALL])
    P.dma("gpsimd", ln_v[:, 3, :], din["ln2b"], writes=[SMALL])
    P.dma("gpsimd", sb_zd.ap(), din["zd"], writes=[IOTA])
    P.dma("gpsimd", sb_ep, din["epow"], writes=[SMALL])
    P.dma("gpsimd", sb_cm64.ap(), din["cm512"], writes=[SMALL])
    P.dma("gpsimd", sb_sml.ap()[:, 12:14], din["bmask"], writes=[SMALL])
    P.dma("gpsimd", sb_o1.ap(), din["ones_bf"], writes=[SMALL])
    P.dma("gpsimd", sb_o1f.ap(), din["ones_f"], writes=[SMALL])

    P.op("scalar", lambda E: E.activation(
        sm[:, S_ETD:S_ETD + 16], sm[:, S_TD:S_TD + 16], AF.Exp),
        reads=[SMALL], writes=[SMALL])
    P.op("vector", lambda E: E.tensor_scalar_mul(
        sm[:, S_NETD:S_NETD + 16], sm[:, S_ETD:S_ETD + 16], -1.0),
        reads=[SMALL], writes=[SMALL])
    P.op("scalar", lambda E: E.activation(
        sm[:, S_LAM:S_LAM + 16], sm[:, S_NETD:S_NETD + 16], AF.Exp),
        reads=[SMALL], writes=[SMALL])
    P.op("scalar", lambda E: E.activation(
        sb_etf16.ap(), sm[:, S_TF:S_TF + 16], AF.Exp),
        reads=[SMALL], writes=[SMALL])
    P.op("vector", lambda E: E.tensor_scalar_mul(
        sb_etf16.ap(), sb_etf16.ap(), sb_bm),
        reads=[SMALL], writes=[SMALL])

    def rows_math(n_tok, sx_pis, sq_pis):
        mu = sb_tmp.ap()[0:1, 0:n_tok]
        va = sb_tmp2.ap()[0:1, 0:n_tok]
        sc = sb_lam.ap()[0:1, 0:n_tok]
        bi = sb_wst.ap()[0:1, 0:n_tok]
        for i, pi in enumerate(sx_pis):
            P.op("vector", lambda E, i=i, pi=pi, mu=mu: E.tensor_scalar_mul(
                mu[0:1, i * 1024:(i + 1) * 1024], POOLS[pi][0:1, :], 1.0 / C),
                reads=[PS[pi]], writes=[TMP])
        for i, pi in enumerate(sq_pis):
            P.op("vector", lambda E, i=i, pi=pi, va=va: E.tensor_scalar_mul(
                va[0:1, i * 1024:(i + 1) * 1024], POOLS[pi][0:1, :], 1.0 / C),
                reads=[PS[pi]], writes=[TMP2])
        P.op("vector", lambda E, sc=sc, mu=mu: E.tensor_mul(sc, mu, mu),
             reads=[TMP, ROWS], writes=[ROWS])
        P.op("vector", lambda E, va=va, sc=sc: E.tensor_sub(va, va, sc),
             reads=[TMP2, ROWS], writes=[TMP2])
        P.op("vector", lambda E, va=va: E.tensor_scalar_add(va, va, EPS),
             reads=[TMP2], writes=[TMP2])
        P.op("scalar", lambda E, va=va: E.activation(va, va, AF.Sqrt),
             reads=[TMP2], writes=[TMP2])
        P.op("vector", lambda E, sc=sc, va=va: E.reciprocal(sc, va),
             reads=[TMP2, ROWS], writes=[ROWS])
        P.op("vector", lambda E, bi=bi, mu=mu, sc=sc: E.tensor_mul(bi, mu, sc),
             reads=[TMP, ROWS, WST[0]], writes=[WST[0]])
        P.op("vector", lambda E, bi=bi: E.tensor_scalar_mul(bi, bi, -1.0),
             reads=[WST[0]], writes=[WST[0]])

    def bcast_rows(n_tok, sc_pi, bi_pi, off=0):
        sc = sb_lam.ap()[0:1, :]
        bi = sb_wst.ap()[0:1, :]
        for s in range(n_tok // 512):
            sl = slice(s * 512, (s + 1) * 512)
            gsl = slice(off + s * 512, off + (s + 1) * 512)
            P.op("tensor", lambda E, sl=sl, gsl=gsl, pi=sc_pi, sc=sc: E.matmul(
                POOLS[pi][:, sl], sb_o1f.ap()[0:1, :], sc[0:1, gsl],
                start=True, stop=True),
                reads=[ROWS, SMALL], writes=[PS[sc_pi]])
            P.op("tensor", lambda E, sl=sl, gsl=gsl, pi=bi_pi, bi=bi: E.matmul(
                POOLS[pi][:, sl], sb_o1f.ap()[0:1, :], bi[0:1, gsl],
                start=True, stop=True),
                reads=[WST[0], SMALL], writes=[PS[bi_pi]])

    x0v = din["x0"]
    for r in range(2):
        t0 = r * AR
        rsl = slice(t0, t0 + AR)
        ROWS = P.barrier_res([ROWS, U, U2])
        for kc in range(KC):
            P.dma("gpsimd", sb_tmp.ap()[:, 0:AR], x0v[kc, :, rsl],
                  writes=[TMP])
            P.op("vector", lambda E: E.tensor_copy(
                sb_bft.ap()[:, 0:AR], sb_tmp.ap()[:, 0:AR]),
                reads=[TMP], writes=[BFT])
            P.op("scalar", lambda E: E.activation(
                sb_bft2.ap()[:, 0:AR], sb_tmp.ap()[:, 0:AR], AF.Square),
                reads=[TMP], writes=[BFT2])
            for s in range(2):
                sl = slice(s * 512, (s + 1) * 512)
                P.op("tensor", lambda E, sl=sl, kc=kc: E.matmul(
                    mm0[0:1, sl], sb_o1.ap(), sb_bft.ap()[:, sl],
                    start=(kc == 0), stop=(kc == KC - 1)),
                    reads=[BFT, SMALL], writes=[PS[0]])
                P.op("tensor", lambda E, sl=sl, kc=kc: E.matmul(
                    ax0[0:1, sl], sb_o1.ap(), sb_bft2.ap()[:, sl],
                    start=(kc == 0), stop=(kc == KC - 1)),
                    reads=[BFT2, SMALL], writes=[PS[2]])
        rows_math(AR, [0], [2])
        bcast_rows(AR, 1, 3)

        for kc in range(KC):
            P.dma("gpsimd", sb_tmp.ap()[:, 0:AR], x0v[kc, :, rsl],
                  writes=[TMP])
            P.op("vector", lambda E: E.tensor_mul(
                sb_tmp2.ap()[:, 0:AR], sb_tmp.ap()[:, 0:AR], mm1[:, :]),
                reads=[TMP, PS[1]], writes=[TMP2])
            P.op("vector", lambda E, kc=kc, rsl=rsl: E.tensor_add(
                st_v[:, kc, rsl], sb_tmp2.ap()[:, 0:AR], ax1[:, :]),
                reads=[TMP2, PS[3]], writes=[AXN[kc]])
            P.dma("gpsimd", d_xn[kc, :, rsl], st_v[:, kc, rsl],
                  reads=[AXN[kc]], writes=[DXN])

        for kc in range(KC):
            P.op("scalar", lambda E, kc=kc, rsl=rsl: E.activation(
                sb_bft2.ap()[:, 0:AR], st_v[:, kc, rsl], AF.Square),
                reads=[AXN[kc]], writes=[BFT2])
            for s in range(2):
                sl = slice(s * 512, (s + 1) * 512)
                gsl = slice(t0 + s * 512, t0 + (s + 1) * 512)
                P.op("tensor", lambda E, sl=sl, kc=kc, gsl=gsl: E.matmul(
                    mm0[0:1, sl], sb_o1.ap(), st_v[:, kc, gsl],
                    start=(kc == 0), stop=(kc == KC - 1)),
                    reads=[AXN[kc], SMALL], writes=[PS[0]])
                P.op("tensor", lambda E, sl=sl, kc=kc: E.matmul(
                    ax0[0:1, sl], sb_o1.ap(), sb_bft2.ap()[:, sl],
                    start=(kc == 0), stop=(kc == KC - 1)),
                    reads=[BFT2, SMALL], writes=[PS[2]])
        rows_math(AR, [0], [2])
        bcast_rows(AR, 1, 3)

        for kc in range(KC):
            P.op("vector", lambda E, kc=kc, rsl=rsl: E.tensor_mul(
                sb_tmp2.ap()[:, 0:AR], st_v[:, kc, rsl], mm1[:, :]),
                reads=[AXN[kc], PS[1]], writes=[TMP2])
            P.op("vector", lambda E: E.tensor_add(
                sb_tmp2.ap()[:, 0:AR], sb_tmp2.ap()[:, 0:AR], ax1[:, :]),
                reads=[TMP2, PS[3]], writes=[TMP2])
            P.op("vector", lambda E, kc=kc, rsl=rsl: E.tensor_scalar(
                ac_v[:, kc, rsl], sb_tmp2.ap()[:, 0:AR],
                ln_v[:, 0, kc:kc + 1], ln_v[:, 1, kc:kc + 1],
                ALU.mult, ALU.add),
                reads=[TMP2, SMALL], writes=[AXL[kc]])

    U = P.barrier_res([U, U2, ROWS])
    for m in range(KC):
        kslab = load_slab(din["wk"], m, 0, KC, 0, 0)
        vslab = load_slab(din["wv"], m, 0, KC, 1, 1)
        ST[m] = P.barrier_res([ST[m], AXN[m]])
        for half in range(2):
            kpool, vpool = (mm0, mm1) if half == 0 else (ax0, ax1)
            kpi, vpi = (0, 1) if half == 0 else (2, 3)
            fns = []
            for kc in range(KC):
                for s in range(2):
                    gsl = slice(half * 1024 + s * 512,
                                half * 1024 + (s + 1) * 512)
                    sl = slice(s * 512, (s + 1) * 512)
                    fns.append(lambda E, kc=kc, sl=sl, gsl=gsl,
                               kp=kpool, ks=kslab: E.matmul(
                                   kp[:, sl], ks[:, kc, :], ac_v[:, kc, gsl],
                                   start=(kc == 0), stop=(kc == KC - 1)))
                for s in range(2):
                    gsl = slice(half * 1024 + s * 512,
                                half * 1024 + (s + 1) * 512)
                    sl = slice(s * 512, (s + 1) * 512)
                    fns.append(lambda E, kc=kc, sl=sl, gsl=gsl,
                               vp=vpool, vs=vslab: E.matmul(
                                   vp[:, sl], vs[:, kc, :], ac_v[:, kc, gsl],
                                   start=(kc == 0), stop=(kc == KC - 1)))
            P.op("tensor", fns, reads=[WSL[0], WSL[1]] + AXL,
                 writes=[PS[kpi], PS[vpi]])
            ktmp, kt_res = (sb_tmp, TMP) if half == 0 else (sb_tmp2, TMP2)
            P.op("scalar", lambda E, kp=kpool, kt=ktmp: E.activation(
                kt.ap()[:, 0:AR], kp[:, :], AF.Exp),
                reads=[PS[kpi]], writes=[kt_res])
            if half == 0:
                P.op("vector", lambda E, vp=vpool, m=m: E.tensor_copy(
                    sb_v016.ap()[:, m:m + 1], vp[:, 0:1]),
                    reads=[PS[vpi]], writes=[SMALL])
            P.op("vector", lambda E, vp=vpool, kt=ktmp, half=half:
                 E.tensor_mul(sb_lam.ap()[:, half * AR:(half + 1) * AR],
                              kt.ap()[:, 0:AR], vp[:, :]),
                 reads=[kt_res, PS[vpi]], writes=[U])
        P.op("vector", lambda E, m=m: E.tensor_tensor_scan(
            st_v[:, m, :],
            sm[:, S_LAM + m:S_LAM + m + 1].broadcast_to([128, TK]),
            sb_lam.ap()[:, 0:TK], 0.0, ALU.mult, ALU.add),
            reads=[U, SMALL], writes=[ST[m]], fifo=False)

    if debug:
        P.dma("gpsimd", dbg["dbg_states"], sb_st.ap(), reads=ST)

    P.dma("gpsimd", cc_in.ap().rearrange("m p -> p m"),
          st_v[:, :, TK - 1:TK].squeeze(), reads=ST, writes=[CCIO], slow=True)
    P.op("gpsimd", lambda E: E.collective_compute(
        "AllGather", ALU.bypass,
        replica_groups=[[0, 1, 2, 3], [4, 5, 6, 7]],
        ins=[cc_in.ap().opt()], outs=[cc_out.ap().opt()]),
        reads=[CCIO], writes=[CCIO], sem="cc")
    if debug:
        P.dma("gpsimd", dbg["dbg_f"], st_v[:, :, TK - 1:TK].squeeze(),
              reads=ST, slow=True)

    ACD = P.barrier_res(AXL)
    if debug:
        P.dma("gpsimd", dbg["dbg_xl"], sb_ac.ap(), reads=[ACD])

    def emit_F_block():
        Fv = sb_F.ap()[:, 0:64].rearrange("p (m r) -> p m r", r=4)
        for rk in range(4):
            P.dma("gpsimd", Fv[:, :, rk],
                  cc_out.ap()[rk].rearrange("m p -> p m"),
                  reads=[CCIO], writes=[FRES], slow=True)
        def spacer():
            P.op("vector", lambda E: E.tensor_copy(
                sb_tmp2.ap()[:, 0:TK], sb_tmp2.ap()[:, 0:TK]),
                reads=[TMP2], writes=[TMP2])
        if debug:
            P.dma("gpsimd", dbg["dbg_Ff"], sb_F.ap()[:, 0:64], reads=[FRES])
        for m in range(KC):
            P.op("scalar", lambda E, m=m: E.activation(
                sb_wA.ap()[:, 4 * m:4 * m + 4], sb_ep, AF.Exp,
                scale=sm[:, S_NETD + m:S_NETD + m + 1]),
                reads=[SMALL], writes=[W4])
        spacer()
        P.op("vector", lambda E: E.tensor_mul(
            sb_wC.ap(), sb_F.ap(), sb_cm64.ap()),
            reads=[FRES, SMALL], writes=[FRES])
        P.op("vector", lambda E: E.tensor_mul(sb_wC.ap(), sb_wC.ap(), sb_wA.ap()),
             reads=[FRES, W4], writes=[FRES])
        spacer()
        P.op("vector", lambda E: E.reduce_sum(
            sb_C16.ap(), sb_wC.ap()[:, 0:64].rearrange("p (m r) -> p m r", r=4),
            AXX),
            reads=[FRES], writes=[SMALL])
        P.op("vector", lambda E: E.tensor_mul(
            sb_boost.ap(), sb_etf16.ap(), sb_v016.ap()),
            reads=[SMALL], writes=[SMALL])
        spacer()
        spacer()
        if debug:
            P.dma("gpsimd", dbg["dbg_C"], sb_C16.ap(), reads=[SMALL])

    LAMF = P.barrier_res([LAM, U, U2, ROWS])

    def mm_full(slab, rhs_fn, par, rhs_res):
        pools = (mm0, mm1) if par == 0 else (ax0, ax1)
        pis = (0, 1) if par == 0 else (2, 3)
        fns = []
        for kc in range(KC):
            for s in range(4):
                pool = pools[s // 2]
                sl = slice((s % 2) * 512, (s % 2 + 1) * 512)
                fns.append(lambda E, kc=kc, s=s, pool=pool, sl=sl, sb=slab:
                           E.matmul(pool[:, sl], sb[:, kc, :], rhs_fn(kc, s),
                                    start=(kc == 0), stop=(kc == KC - 1)))
        return pools, pis, fns

    pend_corr = []
    for m in range(KC):
        if m == 2:
            emit_F_block()
            for f in pend_corr:
                f()
            pend_corr = []
        par = m % 2
        rslab = load_slab(din["wr"], m, 0, KC, par, par * 2, cast_eng="gpsimd")
        pools, pis, fns = mm_full(
            rslab, lambda kc, s: ac_v[:, kc, s * 512:(s + 1) * 512], par, ACD)
        P.op("tensor", fns, reads=[WSL[par * 2], ACD],
             writes=[PS[pis[0]], PS[pis[1]]])
        sig, sres = (sb_bft, BFT) if par == 0 else (sb_bft2, BFT2)
        P.op("scalar", lambda E, p=pools[0], sg=sig: E.activation(
            sg.ap()[:, 0:1024], p[:, :], AF.Sigmoid),
            reads=[PS[pis[0]]], writes=[sres])
        P.op("scalar", lambda E, p=pools[1], sg=sig: E.activation(
            sg.ap()[:, 1024:2048], p[:, :], AF.Sigmoid),
            reads=[PS[pis[1]]], writes=[sres])
        lbuf, lres = (sb_lam, LAMF) if par == 0 else (sb_tmp, TMP)
        P.op("vector", lambda E, m=m, lb=lbuf: E.tensor_tensor_scan(
            lb.ap(), sm[:, S_LAM + m:S_LAM + m + 1].broadcast_to([128, TK]),
            sb_zero.broadcast_to([128, TK]), 1.0, ALU.mult, ALU.add),
            reads=[SMALL], writes=[lres], fifo=False)

        def corr(m=m, lbuf=lbuf, lres=lres, sig=sig, sres=sres):
            P.op("vector", lambda E, m=m, lb=lbuf: E.scalar_tensor_tensor(
                st_v[:, m, :], lb.ap(), sb_C16.ap()[:, m:m + 1],
                sb_st.ap()[:, m * TK:(m + 1) * TK], ALU.mult, ALU.add),
                reads=[lres, SMALL, ST[m]], writes=[ST[m]])
            P.op("vector", lambda E, m=m: E.scalar_tensor_tensor(
                st_v[:, m, :], sb_zd.ap(), sb_boost.ap()[:, m:m + 1],
                sb_st.ap()[:, m * TK:(m + 1) * TK], ALU.mult, ALU.add),
                reads=[SMALL, ST[m]], writes=[ST[m]])
            P.op("vector", lambda E, m=m, sg=sig: E.tensor_mul(
                st_v[:, m, :], st_v[:, m, :], sg.ap()[:, 0:2048]),
                reads=[sres, ST[m]], writes=[ST[m]])

        if m < 2:
            pend_corr.append(corr)
        else:
            corr()
    if debug:
        P.dma("gpsimd", dbg["dbg_rwkv"], sb_st.ap(), reads=ST)

    for m in range(KC):
        par = m % 2
        oslab = load_slab(din["wo"], m, 0, KC, par, par * 2)
        pools, pis, fns = mm_full(
            oslab, lambda kc, s: st_v[:, kc, s * 512:(s + 1) * 512], par, None)
        P.op("tensor", fns, reads=[WSL[par * 2]] + ST,
             writes=[PS[pis[0]], PS[pis[1]]])
        xnb, xres = (sb_bft, BFT) if par == 0 else (sb_bft2, BFT2)
        P.dma("gpsimd", xnb.ap()[:, 0:2048], d_xn[m, :, :],
              reads=[DXN], writes=[xres])
        a1t, ares = (sb_tmp, TMP) if par == 0 else (sb_tmp2, TMP2)
        P.op("vector", lambda E, p=pools[0], a=a1t, xb=xnb: E.tensor_add(
            a.ap()[:, 0:1024], p[:, :], xb.ap()[:, 0:1024]),
            reads=[PS[pis[0]], xres], writes=[ares])
        P.op("vector", lambda E, p=pools[1], a=a1t, xb=xnb: E.tensor_add(
            a.ap()[:, 1024:2048], p[:, :], xb.ap()[:, 1024:2048]),
            reads=[PS[pis[1]], xres], writes=[ares])
        P.dma("gpsimd", d_a1[m, :, :], a1t.ap()[:, 0:2048],
              reads=[ares], writes=[DA1])

    STF = P.barrier_res(ST)
    for kc in range(KC):
        P.dma("gpsimd", sb_tmp.ap()[:, 0:2048], d_a1[kc, :, :],
              reads=[DA1], writes=[TMP])
        P.op("vector", lambda E, kc=kc: E.tensor_copy(
            st_v[:, kc, :], sb_tmp.ap()[:, 0:2048]),
            reads=[TMP], writes=[STF])
        P.op("scalar", lambda E: E.activation(
            sb_bft2.ap()[:, 0:2048], sb_tmp.ap()[:, 0:2048], AF.Square),
            reads=[TMP], writes=[BFT2])
        for s in range(4):
            pxi = s // 2
            sl = slice((s % 2) * 512, (s % 2 + 1) * 512)
            tsl = slice(s * 512, (s + 1) * 512)
            P.op("tensor", lambda E, pxi=pxi, sl=sl, kc=kc, tsl=tsl: E.matmul(
                POOLS[pxi][0:1, sl], sb_o1.ap(), st_v[:, kc, tsl],
                start=(kc == 0), stop=(kc == KC - 1)),
                reads=[STF, SMALL], writes=[PS[pxi]])
            P.op("tensor", lambda E, pxi=pxi, sl=sl, kc=kc, tsl=tsl: E.matmul(
                POOLS[2 + pxi][0:1, sl], sb_o1.ap(), sb_bft2.ap()[:, tsl],
                start=(kc == 0), stop=(kc == KC - 1)),
                reads=[BFT2, SMALL], writes=[PS[2 + pxi]])
    ROWS = P.barrier_res([ROWS, LAMF])
    rows_math(2048, [0, 1], [2, 3])
    for th in range(2):
        tsl = slice(th * 1024, (th + 1) * 1024)
        bcast_rows(1024, 0, 2, off=th * 1024)
        for kc in range(KC):
            P.op("vector", lambda E, kc=kc, tsl=tsl: E.tensor_mul(
                sb_tmp.ap()[:, 0:1024], st_v[:, kc, tsl], mm0[:, :]),
                reads=[STF, PS[0]], writes=[TMP])
            P.op("vector", lambda E: E.tensor_add(
                sb_tmp.ap()[:, 0:1024], sb_tmp.ap()[:, 0:1024], ax0[:, :]),
                reads=[TMP, PS[2]], writes=[TMP])
            P.op("vector", lambda E, kc=kc, tsl=tsl: E.tensor_scalar(
                st_v[:, kc, tsl], sb_tmp.ap()[:, 0:1024],
                ln_v[:, 2, kc:kc + 1], ln_v[:, 3, kc:kc + 1],
                ALU.mult, ALU.add),
                reads=[TMP, SMALL], writes=[STF])
    if debug:
        P.dma("gpsimd", dbg["dbg_xl2"], sb_st.ap(), reads=[STF])

    for h in range(HC):
        par = h % 2
        fslab = load_slab(din["wfk"], h, 0, KC, par, par * 2)
        pools, pis, fns = mm_full(
            fslab, lambda kc, s: st_v[:, kc, s * 512:(s + 1) * 512], par, None)
        P.op("tensor", fns, reads=[WSL[par * 2], STF],
             writes=[PS[pis[0]], PS[pis[1]]])
        tmp, tres = (sb_tmp, TMP) if par == 0 else (sb_tmp2, TMP2)
        hb, hres = (sb_bft, BFT) if par == 0 else (sb_bft2, BFT2)
        P.op("scalar", lambda E, p=pools[0], t=tmp: E.activation(
            t.ap()[:, 0:1024], p[:, :], AF.Relu),
            reads=[PS[pis[0]]], writes=[tres])
        P.op("scalar", lambda E, p=pools[1], t=tmp: E.activation(
            t.ap()[:, 1024:2048], p[:, :], AF.Relu),
            reads=[PS[pis[1]]], writes=[tres])
        P.op("vector", lambda E, t=tmp, hb=hb: E.tensor_mul(
            hb.ap()[:, 0:2048], t.ap()[:, 0:2048], t.ap()[:, 0:2048]),
            reads=[tres], writes=[hres])
        P.dma("gpsimd", d_h2[h, :, :], hb.ap()[:, 0:2048],
              reads=[hres], writes=[DH2])

    ACF = P.barrier_res([ACD])
    for kc in range(KC):
        P.dma(("gpsimd", "scalar")[kc % 2], ac_v[:, kc, :], d_xn[kc, :, :],
              reads=[DXN], writes=[ACF])
    for h in range(HC):
        par = h % 2
        gslab = load_slab(din["wg"], h, 0, KC, 0, par * 2 + 0)
        uslab = load_slab(din["wu"], h, 0, KC, 1, par * 2 + 1)
        for th in range(2):
            pidx = (h * 2 + th) % 2
            gpool, upool = (mm0, ax0) if pidx == 0 else (mm1, ax1)
            gpi, upi = (0, 2) if pidx == 0 else (1, 3)
            fns = []
            for kc in range(KC):
                for s in range(2):
                    sl = slice(s * 512, (s + 1) * 512)
                    rsl = slice(th * 1024 + s * 512, th * 1024 + (s + 1) * 512)
                    fns.append(lambda E, kc=kc, sl=sl, rsl=rsl, gp=gpool,
                               gs=gslab: E.matmul(
                                   gp[:, sl], gs[:, kc, :], ac_v[:, kc, rsl],
                                   start=(kc == 0), stop=(kc == KC - 1)))
                for s in range(2):
                    sl = slice(s * 512, (s + 1) * 512)
                    rsl = slice(th * 1024 + s * 512, th * 1024 + (s + 1) * 512)
                    fns.append(lambda E, kc=kc, sl=sl, rsl=rsl, up=upool,
                               us=uslab: E.matmul(
                                   up[:, sl], us[:, kc, :], ac_v[:, kc, rsl],
                                   start=(kc == 0), stop=(kc == KC - 1)))
            P.op("tensor", fns,
                 reads=[WSL[par * 2], WSL[par * 2 + 1], ACF],
                 writes=[PS[gpi], PS[upi]])
            tmp, tres = (sb_tmp, TMP) if pidx == 0 else (sb_tmp2, TMP2)
            hb, hres = (sb_bft, BFT) if pidx == 0 else (sb_bft2, BFT2)
            P.op("scalar", lambda E, p=gpool, t=tmp: E.activation(
                t.ap()[:, 0:1024], p[:, :], AF.Silu),
                reads=[PS[gpi]], writes=[tres])
            P.op("vector", lambda E, p=upool, t=tmp, hb=hb: E.tensor_mul(
                hb.ap()[:, 0:1024], t.ap()[:, 0:1024], p[:, :]),
                reads=[tres, PS[upi]], writes=[hres])
            P.dma("gpsimd", d_gated[h, :, th * 1024:(th + 1) * 1024],
                  hb.ap()[:, 0:1024], reads=[hres], writes=[DGATED])

    LAMF = P.barrier_res([LAMF, ROWS])
    STH = P.barrier_res([STF])
    ACH = P.barrier_res([ACF])

    def hidden_mm(wdram, m, rhs_fn, pool, pool_pi):
        for piece in range(4):
            slot = piece % 4
            slab = load_slab(wdram, m, piece * 16, 16, piece % 2, slot)
            fns = []
            for k2 in range(16):
                kc = piece * 16 + k2
                for s in range(2):
                    sl = slice(s * 512, (s + 1) * 512)
                    fns.append(lambda E, kc=kc, k2=k2, sl=sl, sb=slab,
                               p=pool: E.matmul(
                                   p[:, sl], sb[:, k2, :], rhs_fn(kc, sl),
                                   start=(kc == 0), stop=(kc == HC - 1)))
            P.op("tensor", fns, reads=[WSL[slot], STH, ACH],
                 writes=[PS[pool_pi]])

    def gated_rhs(kc, sl):
        area = st_h if kc < 32 else ac_h
        return area[:, kc % 32, sl]

    TILE_QS = ["gpsimd", "sync", "scalar"]
    for th in range(2):
        for h in range(HC):
            area = st_h if h < 32 else ac_h
            P.dma(TILE_QS[h % 3], area[:, h % 32, :],
                  d_gated[h, :, th * 1024:(th + 1) * 1024],
                  reads=[DGATED], writes=[STH if h < 32 else ACH])
        for m in range(KC):
            pidx = m % 2
            pool = mm0 if pidx == 0 else mm1
            hidden_mm(din["wd"], m, gated_rhs, pool, pidx)
            off = pidx * 1024
            P.op("scalar", lambda E, p=pool, off=off: E.activation(
                sb_lam.ap()[:, off:off + 1024], p[:, :], AF.Silu),
                reads=[PS[pidx]], writes=[LAMF])
            P.dma("gpsimd", d_g[m, :, th * 1024:(th + 1) * 1024],
                  sb_lam.ap()[:, off:off + 1024], reads=[LAMF], writes=[DG])

    for th in range(2):
        for h in range(HC):
            area = st_h if h < 32 else ac_h
            P.dma(TILE_QS[h % 3], area[:, h % 32, :],
                  d_h2[h, :, th * 1024:(th + 1) * 1024],
                  reads=[DH2], writes=[STH if h < 32 else ACH])
        for m in range(KC):
            pidx = m % 2
            pool = mm0 if pidx == 0 else mm1
            hidden_mm(din["wfv"], m, gated_rhs, pool, pidx)
            a1t, ares = (sb_tmp, TMP) if pidx == 0 else (sb_tmp2, TMP2)
            x0b = x0b_ap[pidx]
            goff = pidx * 1024
            P.dma("gpsimd", a1t.ap()[:, 0:1024],
                  d_a1[m, :, th * 1024:(th + 1) * 1024],
                  reads=[DA1], writes=[ares])
            P.dma("scalar", sb_lam.ap()[:, goff:goff + 1024],
                  d_g[m, :, th * 1024:(th + 1) * 1024],
                  reads=[DG], writes=[LAMF])
            P.dma("sync", x0b,
                  x0v[m, :, th * 1024:(th + 1) * 1024], writes=[ares])
            P.op("vector", lambda E, p=pool, a=a1t: E.tensor_add(
                a.ap()[:, 0:1024], p[:, :], a.ap()[:, 0:1024]),
                reads=[PS[pidx], ares], writes=[ares])
            P.op("vector", lambda E, a=a1t, goff=goff: E.tensor_mul(
                a.ap()[:, 0:1024], a.ap()[:, 0:1024],
                sb_lam.ap()[:, goff:goff + 1024]),
                reads=[ares, LAMF], writes=[ares])
            P.op("vector", lambda E, a=a1t, xb=x0b: E.tensor_add(
                a.ap()[:, 0:1024], a.ap()[:, 0:1024], xb),
                reads=[ares], writes=[ares])
            P.dma("sync", o_x[m, :, th * 1024:(th + 1) * 1024],
                  a1t.ap()[:, 0:1024], reads=[ares])

    if debug:
        P.dma("gpsimd", dbg["dbg_xn"], d_xn.flatten(), reads=[DXN])
    if debug2:
        P.dma("gpsimd", dbg["d2_ccin"], cc_in.ap(), reads=[CCIO])
        P.dma("gpsimd", dbg["d2_ccout"], cc_out.ap(), reads=[CCIO])
        P.dma("gpsimd", dbg["d2_C16"], sb_C16.ap(), reads=[SMALL])
        P.dma("gpsimd", dbg["d2_Ff"], sb_F.ap()[:, 0:64], reads=[FRES])
        P.dma("gpsimd", dbg["d2_wA"], sb_wA.ap()[:, 0:64], reads=[W4])
        P.dma("gpsimd", dbg["d2_wC"], sb_wC.ap()[:, 0:64], reads=[FRES])
        P.dma("gpsimd", dbg["d2_netd"], sm[:, S_NETD:S_NETD + 16], reads=[SMALL])

    P.stream["sync"].append((
        [lambda E: E.nop()],
        [(s, P.tot[s]) for s in P.tot if P.tot[s] > 0], None, 0))

    with nc.Block() as block:
        def run(E, name):
            for fns, waits, sem_, by in P.stream[name]:
                for s, v in waits:
                    E.wait_ge(sems[s], v)
                inst = None
                for fn in fns:
                    inst = fn(E)
                if sem_ is not None and by > 0:
                    inst.then_inc(sems[sem_], by)

        @block.sync
        def _(E):
            run(E, "sync")

        @block.tensor
        def _(E):
            run(E, "tensor")

        @block.vector
        def _(E):
            run(E, "vector")

        @block.scalar
        def _(E):
            run(E, "scalar")

        @block.gpsimd
        def _(E):
            run(E, "gpsimd")

    stack.close()
    return nc


def _tile_w(W):
    K, M = W.shape
    t = W.reshape(K // 128, 128, M // 128, 128)
    return np.ascontiguousarray(t.transpose(2, 1, 0, 3))


def _vec128(v):
    return np.ascontiguousarray(np.asarray(v, np.float32).reshape(16, 128).T)


def kernel(**inputs):
    f32 = np.float32
    x = np.asarray(inputs["L0_attn_x"], f32)
    td = np.asarray(inputs["L0_attn_time_decay"], f32).reshape(-1)
    tf = np.asarray(inputs["L0_attn_time_first"], f32).reshape(-1)
    shared = {
        "wk": _tile_w(np.asarray(inputs["L0_attn_Wk"], f32)),
        "wv": _tile_w(np.asarray(inputs["L0_attn_Wv"], f32)),
        "wr": _tile_w(np.asarray(inputs["L0_attn_Wr"], f32)),
        "wo": _tile_w(np.asarray(inputs["L0_attn_Wo"], f32)),
        "wfk": _tile_w(np.asarray(inputs["L0_attn_Wfk"], f32)),
        "wfv": _tile_w(np.asarray(inputs["L0_attn_Wfv"], f32)),
        "wg": _tile_w(np.ascontiguousarray(
            np.asarray(inputs["L0_ffn_w_gate"], f32).T)),
        "wu": _tile_w(np.ascontiguousarray(
            np.asarray(inputs["L0_ffn_w_up"], f32).T)),
        "wd": _tile_w(np.ascontiguousarray(
            np.asarray(inputs["L0_ffn_w_down"], f32).T)),
        "td": _vec128(td), "tf": _vec128(tf),
        "ln1w": _vec128(inputs["L0_attn_ln1_w"]),
        "ln1b": _vec128(inputs["L0_attn_ln1_b"]),
        "ln2w": _vec128(inputs["L0_attn_ln2_w"]),
        "ln2b": _vec128(inputs["L0_attn_ln2_b"]),
        "zd": np.ascontiguousarray(np.concatenate(
            [np.ones((128, 1)), np.zeros((128, TK - 1))], axis=1)
            .astype(ml_dtypes.bfloat16)),
        "ones_bf": np.ones((128, 1), ml_dtypes.bfloat16),
        "ones_f": np.ones((33, 128), f32),
    }

    in_maps = []
    for b in range(2):
        for q in range(4):
            x0c = np.ascontiguousarray(x[b, q * TK:(q + 1) * TK, :].T)
            epow = np.zeros((128, 4), f32)
            cmask = np.zeros((128, 4), f32)
            for p in range(q):
                epow[:, p] = 2048.0 * (q - 1 - p)
                cmask[:, p] = 1.0
            m = dict(shared)
            m["x0"] = x0c.reshape(KC, 128, TK)
            m["epow"] = epow
            cm512 = np.zeros((128, 512), f32)
            cm512[:, 0:64] = np.tile(cmask, (1, KC))
            m["cm512"] = cm512.astype(ml_dtypes.bfloat16)
            bm2 = np.zeros((128, 2), f32)
            bm2[:, 0] = 1.0 if q == 0 else 0.0
            m["bmask"] = bm2
            in_maps.append(m)

    nc = build_program(debug=DEBUG, debug2=DEBUG2)
    res = run_bass_kernel_spmd(nc, in_maps, core_ids=list(range(8)))
    out = np.empty((2, 8192, C), f32)
    for b in range(2):
        for q in range(4):
            r = np.asarray(res.results[b * 4 + q]["o_x"]).reshape(C, TK)
            out[b, q * TK:(q + 1) * TK, :] = r.T
    if DEBUG or DEBUG2:
        kernel.last_results = res
    return out
